# version 36
# baseline (speedup 1.0000x reference)
import contextlib

import numpy as np
import ml_dtypes

import concourse.bass as bass
import concourse.mybir as mybir
from concourse.bass_utils import run_bass_kernel_spmd

F32 = mybir.dt.float32
BF16 = mybir.dt.bfloat16
I32 = mybir.dt.int32
U32 = mybir.dt.uint32
AF = mybir.ActivationFunctionType
ALU = mybir.AluOpType
AX = mybir.AxisListType
DS = bass.ds
IOA = bass.IndirectOffsetOnAxis

ENGINES = ("sync", "gpsimd", "tensor", "vector", "scalar")


class Buf:

    __slots__ = ("name", "w", "r")

    def __init__(self, name):
        self.name = name
        self.w = {}
        self.r = {}


class Prog:

    def __init__(self, nc):
        self.nc = nc
        self.items = {e: [] for e in ENGINES}
        self.counts = {}
        self.qwm = {e: {} for e in ENGINES}

    def bar(self, sem):
        return (sem, self.counts.get(sem, 0))

    def _mkwaits(self, eng, reads, writes, waits):
        w = {}

        def add(t):
            if t is None:
                return
            s, c = t
            if w.get(s, 0) < c:
                w[s] = c

        for t in waits:
            add(t)
        for b in reads:
            for it in b.w.items():
                add(it)
        for b in writes:
            for it in b.w.items():
                add(it)
            for it in b.r.items():
                add(it)
        wm = self.qwm[eng]
        for s in list(w):
            c = w[s]
            if s.startswith("d_"):
                if c <= wm.get(s, 0):
                    del w[s]
                    continue
                assert c == self.counts.get(s, 0), (
                    f"non-barrier wait on DMA sem {s}: {c} != {self.counts.get(s, 0)}"
                )
            if wm.get(s, 0) < c:
                wm[s] = c
        return w

    def _record(self, reads, writes, tok):
        for b in reads:
            if b.r.get(tok[0], 0) < tok[1]:
                b.r[tok[0]] = tok[1]
        for b in writes:
            if b.w.get(tok[0], 0) < tok[1]:
                b.w[tok[0]] = tok[1]
            b.r = {}

    def op(self, eng, fn, reads=(), writes=(), waits=(), dma=False, sem=None, inc=True):
        sem = sem or (("d_" if dma else "s_") + eng)
        amt = 16 if dma else 1
        w = self._mkwaits(eng, reads, writes, waits)
        tok = None
        if inc:
            self.counts[sem] = self.counts.get(sem, 0) + amt
            tok = (sem, self.counts[sem])
        if reads or writes:
            assert tok is not None
        self._record(reads, writes, tok) if tok else None
        self.items[eng].append((fn, w, sem if inc else None, amt))
        return tok

    def grp(self, eng, fns, reads=(), writes=(), waits=()):
        sem = "s_" + eng
        w = self._mkwaits(eng, reads, writes, waits)
        for k, fn in enumerate(fns):
            if k == len(fns) - 1:
                self.counts[sem] = self.counts.get(sem, 0) + 1
                tok = (sem, self.counts[sem])
                self.items[eng].append((fn, w if len(fns) == 1 else {}, sem, 1))
            else:
                self.items[eng].append((fn, w if k == 0 else {}, None, 1))
        self._record(reads, writes, tok)
        return tok

    def settle(self, buf, sem):
        buf.w = {sem: self.counts.get(sem, 0)}

    def build(self, stack):
        nc = self.nc
        used = set()
        for e in ENGINES:
            for _, w, s, _ in self.items[e]:
                if s:
                    used.add(s)
                used.update(w.keys())
        sems = {s: stack.enter_context(nc.semaphore(s)) for s in sorted(used)}
        block = stack.enter_context(nc.Block())

        def mk(eng_name):
            items = self.items[eng_name]

            def section(eng):
                seen = {}
                for fn, waits, inc_sem, amt in items:
                    for s, c in waits.items():
                        if seen.get(s, 0) < c:
                            eng.wait_ge(sems[s], c)
                            seen[s] = c
                    inst = fn(eng)
                    if inc_sem:
                        inst.then_inc(sems[inc_sem], amt)

            return section

        for e in ENGINES:
            if self.items[e]:
                getattr(block, e)(mk(e))


def build_program(nc, T=4096, SEG=256):
    C, HS, E, H = 1024, 64, 32, 256
    NT = T // 128
    NC8 = C // 128
    NSEG = T // SEG + E
    SQ = SEG // 128
    SROWS = NSEG * SEG
    NPT = T // SEG
    OOB = float(1 << 25)

    dt = nc.dram_tensor
    x_d = dt("x", [T, C], F32, kind="ExternalInput").ap()
    qkvw_d = dt("qkv_w", [192, C], F32, kind="ExternalInput").ap()
    qkvb_d = dt("qkv_b", [1, 192], F32, kind="ExternalInput").ap()
    projw_d = dt("proj_w", [C, C], F32, kind="ExternalInput").ap()
    projb_d = dt("proj_b", [1, C], F32, kind="ExternalInput").ap()
    rw_d = dt("router_w", [E, C], F32, kind="ExternalInput").ap()
    rb_d = dt("router_b", [1, E], F32, kind="ExternalInput").ap()
    WPC = 4 * C + 2 + NC8
    wpk_d = dt("wpk", [E * 128, WPC], F32, kind="ExternalInput").ap()
    b2r_d = dt("b2row", [E, C], F32, kind="ExternalInput").ap()
    identbf_d = dt("identbf", [128, 128], BF16, kind="ExternalInput").ap()
    identf_d = dt("identf", [128, 128], F32, kind="ExternalInput").ap()
    utmask_d = dt("utmask", [128, 128], BF16, kind="ExternalInput").ap()
    iota32_d = dt("iota32", [128, E], F32, kind="ExternalInput").ap()
    iotaE_d = dt("iotaE", [E, 1], F32, kind="ExternalInput").ap()
    iotaP_d = dt("iotaP", [128, 1], F32, kind="ExternalInput").ap()
    u32_d = dt("u32", [E, E], F32, kind="ExternalInput").ap()
    ustrict_d = dt("ustrict", [128, 128], BF16, kind="ExternalInput").ap()
    thrP_d = dt("thrP", [E, NPT], F32, kind="ExternalInput").ap()
    thrS_d = dt("thrS", [E, NSEG], F32, kind="ExternalInput").ap()
    out_d = dt("out", [T, C], F32, kind="ExternalOutput").ap()
    tokhl_d = dt("tokhl", [128, 2 * NT], BF16, kind="ExternalInput").ap()
    CW = C + 2
    xnsort_d = dt("xnsort", [SROWS, CW], BF16, kind="Internal").ap()
    ydst_d = dt("ydst", [T, C], F32, kind="Internal").ap()

    st = contextlib.ExitStack()
    sb = lambda name, shape, dty: st.enter_context(nc.sbuf_tensor(name, shape, dty))
    ps = lambda name, shape, dty: st.enter_context(nc.psum_tensor(name, shape, dty))

    c_identbf = sb("c_identbf", [128, 128], BF16)
    c_identf = sb("c_identf", [128, 128], F32)
    c_utmask = sb("c_utmask", [128, 128], BF16)
    c_iota32 = sb("c_iota32", [128, E], F32)
    c_iotaE = sb("c_iotaE", [E, 1], F32)
    c_u32 = sb("c_u32", [E, E], F32)
    c_ustrict = sb("c_ustrict", [128, 128], BF16)
    c_thrP = sb("c_thrP", [E, NPT], F32)
    c_thrS = sb("c_thrS", [E, NSEG], F32)
    c_onescol = sb("c_onescol", [128, 1], F32)
    c_onescolb = sb("c_onescolb", [128, 1], BF16)
    c_ones1f = sb("c_ones1f", [1, 128], F32)
    c_ones1b = sb("c_ones1b", [1, max(SEG, 512)], BF16)
    c_eps = sb("c_eps", [128, 1], F32)

    wqk = sb("wqk", [128, NC8 * 128], BF16)
    wv = sb("wv", [128, NC8 * 64], BF16)
    wrt = sb("wrt", [128, NC8 * E], BF16)
    pefft = sb("pefft", [128, C], BF16)
    projb_b = sb("projb_b", [1, C], BF16)
    qb = sb("qb", [64, 1], F32)
    kb = sb("kb", [64, 1], F32)
    vb = sb("vb", [64, 1], F32)
    rbrep = sb("rbrep", [128, E], F32)

    bstg = sb("bstg", [1, 192], F32)

    xall = sb("xall", [128, NT * CW], BF16)
    OT = sb("OTbuf", [128, 256], BF16)
    ohall = sb("ohall", [128, NT * E], BF16)
    rank32 = sb("rank32", [128, NT], I32)
    runrow = sb("runrow", [1, NT * E], F32)
    runcur = sb("runcur", [1, E], F32)
    ytm = [sb(f"ytm{p}", [128, SQ * C], F32) for p in range(2)]
    if SQ >= 2:
        xld = [ytm[p // 2][:, C * (p % 2):C * (p % 2 + 1)] for p in range(4)]
    else:
        xld = [sb(f"xld{p}", [128, C], F32)[:] for p in range(4)]
    stat = [sb(f"stat{p}", [128, 8], F32) for p in range(4)]
    xnT = [sb(f"xnT{p}", [128, C], BF16) for p in range(2)]
    qkvt = [sb(f"qkvt{p}", [64, 256], BF16) for p in range(4)]
    kvbrow = sb("kvbrow", [1, 128], BF16)
    ktv = [sb(f"ktv{p}", [128, 128], BF16) for p in range(4)]
    A_bf = [sb(f"A_bf{p}", [128, 128], BF16) for p in range(2)]
    S32 = sb("S32", [64, 64], F32)
    Sbf = sb("Sbf", [64, 64], BF16)
    logit = sb("logit", [128, E], F32)
    mx8 = sb("mx8", [128, 8], F32)
    mi8 = sb("mi8", [128, 8], U32)
    ecol = sb("ecol", [128, 1], F32)

    cntcol = sb("cntcol", [E, 1], F32)
    grid = sb("grid", [E, NPT], F32)
    padded = sb("padded", [E, 1], F32)
    baser = sb("baser", [1, E], F32)
    basec = sb("basec", [E, 1], F32)
    bpc = sb("bpc", [E, 1], F32)
    segoh = sb("segoh", [E, NSEG], F32)
    segtmp = sb("segtmp", [E, NSEG], F32)
    seger = sb("seger", [1, NSEG], F32)
    segliv = sb("segliv", [1, NSEG], F32)
    segdead = sb("segdead", [1, NSEG], F32)
    segoff = sb("segoff", [1, NSEG], F32)
    idxf = sb("idxf", [128, NSEG], F32)
    idxw = sb("idxw", [128, NSEG], I32)
    c_iotaP = sb("c_iotaP", [128, 1], F32)
    brt = sb("brt", [1, E], F32)
    maskd = sb("maskd", [128, E], F32)
    rankf = sb("rankf", [128, 1], F32)

    wpks = [sb(f"wpks{p}", [128, WPC], BF16) for p in range(4)]
    bcol = [sb(f"bcol{p}", [128, 2], F32) for p in range(4)]
    xns = [sb(f"xns{p}", [128, SQ * CW], BF16) for p in range(4)]
    xnsT = [sb(f"xnsT{p}", [128, SQ * C], BF16) for p in range(2)]
    hfm = [sb(f"hfm{p}", [128, 2 * SEG], BF16) for p in range(2)]

    b2r = [sb(f"b2r{p}", [2, C], BF16) for p in range(4)]
    idxe = sb("idxe", [2, NSEG], I32)
    c_tokhl = sb("c_tokhl", [128, 2 * NT], BF16)
    tokid = [sb(f"tokid{p}", [128, SQ], I32) for p in range(4)]
    tokf = [sb(f"tokf{p}", [128, SQ], F32) for p in range(4)]
    tokf2 = [sb(f"tokf2{p}", [128, 2 * SQ], F32) for p in range(4)]
    xb2 = [sb(f"xb2{p}", [128, C], F32) for p in range(3)]
    scr1 = sb("scr1", [128, C], F32)
    outt = [sb(f"outt{p}", [128, C], F32) for p in range(2)]
    wstg = [scr1, outt[1]]
    lndump = outt[0]
    wstgb = sb("wstgb", [128, C], BF16)
    attnp = [outt[1], scr1]

    B0 = ps("B0", [128, 512], F32)
    B1 = ps("B1", [128, 512], F32)
    B2 = ps("B2", [128, 512], F32)
    B3 = ps("B3", [128, 512], F32)
    B4 = ps("B4", [128, 512], F32)
    B5 = ps("B5", [128, 512], F32)
    B6 = ps("B6", [128, 1024], BF16)
    B7 = ps("B7", [128, 1024], BF16)

    P = Prog(nc)
    nb = lambda n: Buf(n)

    bC = nb("consts")
    bW = nb("wpro")
    bStg = [nb("wstg0"), nb("wstg1")]
    bLnd = nb("lndump")
    bStgb = nb("wstgb")
    bBstg = nb("bstg")
    bXall = [nb(f"xall{i}") for i in range(NT)]
    bOT = [nb(f"OT{i}") for i in range(4)]
    bOh = [nb(f"oh{i}") for i in range(NT)]
    bRk = [nb(f"rk{i}") for i in range(NT)]
    bRun = [nb(f"run{i}") for i in range(NT)]
    bRunc = nb("runcur")
    bXld = [nb(f"xld{p}") for p in range(4)]
    bStat = [nb(f"stat{p}") for p in range(4)]
    bXnT = [nb(f"xnT{p}") for p in range(2)]
    bKvb = nb("kvbrow")
    bQkvt = [nb(f"qkvt{p}") for p in range(4)]
    bKtv = [nb(f"ktv{p}") for p in range(4)]
    bA = [nb(f"A_bf{p}") for p in range(2)]
    bS32 = nb("S32")
    bSbf = nb("Sbf")
    bLog = nb("logit")
    bMx = nb("mx8")
    bMi = nb("mi8")
    bEc = nb("ecol")
    bSort = nb("sorttabs")
    bIdxw = nb("idxw")
    bBrt = nb("brt")
    bMaskd = nb("maskd")
    bRankf = nb("rankf")
    bWpk = [nb(f"wpks{p}") for p in range(4)]
    bBcol = [nb(f"bcol{p}") for p in range(4)]
    bXns = [nb(f"xns{p}") for p in range(4)]
    bXnsT = [nb(f"xnsT{p}") for p in range(2)]
    bHfm = [nb(f"hfm{p}") for p in range(2)]
    bYtm = [nb(f"ytm{p}") for p in range(2)]
    bB2r = [nb(f"b2r{p}") for p in range(4)]
    bTk = [nb(f"tokid{p}") for p in range(4)]
    bXb2 = [nb(f"xb2{p}") for p in range(3)]
    bAttnp = [bStg[1], bStg[0]]
    pQkv = [nb(f"pQkv{p}") for p in range(2)]
    pA = [nb(f"pA{p}") for p in range(2)]
    pO = nb("pO")
    pS = nb("pS")
    pCnt = nb("pCnt")
    pSrt = nb("pSrt")
    pRank = nb("pRank")
    pIdx = nb("pIdx")
    pPeff = nb("pPeff")
    pXnT = [nb(f"pXnT{p}") for p in range(2)]
    pH = [nb(f"pH{p}") for p in range(2)]
    pY = [nb(f"pY{p}") for p in range(2)]

    for tgt, src in [
        (c_identbf, identbf_d), (c_identf, identf_d), (c_utmask, utmask_d),
        (c_iota32, iota32_d), (c_iotaE, iotaE_d), (c_u32, u32_d), (c_iotaP, iotaP_d),
        (c_ustrict, ustrict_d), (c_thrP, thrP_d), (c_thrS, thrS_d),
        (c_tokhl, tokhl_d),
    ]:
        P.op("sync", lambda s, a=tgt, b=src: s.dma_start(out=a[:], in_=b),
             writes=[bC], dma=True, sem="d_c")
    P.op("vector", lambda v: v.memset(c_onescol[:], 1.0), writes=[bC])
    P.op("vector", lambda v: v.memset(c_onescolb[:], 1.0), writes=[bC])
    P.op("vector", lambda v: v.memset(c_ones1f[:], 1.0), writes=[bC])
    P.op("vector", lambda v: v.memset(c_ones1b[:], 1.0), writes=[bC])
    P.op("vector", lambda v: v.memset(c_eps[:], 1e-5), writes=[bC])
    P.op("vector", lambda v: v.memset(runcur[:], 0.0), writes=[bRunc])
    P.op("vector", lambda v: v.memset(S32[:], 0.0), writes=[bS32])
    P.op("vector", lambda v: v.memset(Sbf[:], 0.0), writes=[bSbf])

    def x_load(i):
        p = i % 4
        P.op("sync", lambda s, i=i, p=p: s.dma_start(
            out=xld[p], in_=x_d[128 * i:128 * (i + 1), :]),
            writes=[bXld[p]], dma=True, sem=f"d_x{p}")
    for k in range(min(4, NT)):
        x_load(k)

    P.op("gpsimd", lambda g: g.dma_start(out=wstgb[:], in_=qkvw_d[0:128, :]),
         writes=[bStgb], dma=True, sem="d_wg")
    P.grp("tensor", [
        (lambda t, j=j: t.transpose(
            B6[:, 128 * j:128 * (j + 1)], wstgb[:, 128 * j:128 * (j + 1)],
            c_identbf[:])) for j in range(NC8)],
        reads=[bStgb, bC], writes=[pXnT[0]])
    P.op("vector", lambda v: v.tensor_copy(wqk[:], B6[:]), reads=[pXnT[0]], writes=[bW])
    P.op("gpsimd", lambda g: g.dma_start(out=wstgb[0:64, :], in_=qkvw_d[128:192, :]),
         writes=[bStgb], dma=True, sem="d_wg")
    P.grp("tensor", [
        (lambda t, j=j: t.transpose(
            B6[:, 64 * j:64 * (j + 1)], wstgb[0:64, 128 * j:128 * (j + 1)],
            c_identbf[0:64, 0:64])) for j in range(NC8)],
        reads=[bStgb, bC], writes=[pXnT[0]])
    P.op("vector", lambda v: v.tensor_copy(wv[:], B6[:, 0:NC8 * 64]),
         reads=[pXnT[0]], writes=[bW])
    P.op("gpsimd", lambda g: g.dma_start(out=wstgb[0:E, :], in_=rw_d),
         writes=[bStgb], dma=True, sem="d_wg")
    P.grp("tensor", [
        (lambda t, j=j: t.transpose(
            B6[:, E * j:E * (j + 1)], wstgb[0:E, 128 * j:128 * (j + 1)],
            c_identbf[0:E, 0:E])) for j in range(NC8)],
        reads=[bStgb, bC], writes=[pXnT[0]])
    P.op("vector", lambda v: v.tensor_copy(wrt[:], B6[:, 0:NC8 * E]),
         reads=[pXnT[0]], writes=[bW])
    for cj in range(NC8):
        wp = cj % 2
        P.op("sync", lambda s, cj=cj, wp=wp: s.dma_start(
            out=wstg[wp][:], in_=projw_d[128 * cj:128 * (cj + 1), :]),
            writes=[bStg[wp]], dma=True, sem=f"d_pw{wp}")
        P.grp("tensor", [
            (lambda t, wp=wp, k=k: t.matmul(
                B4[0:64, 0:128], wstg[wp][:, 64 * k:64 * (k + 1)], c_identf[:],
                is_transpose=True, start=(k == 0), stop=(k == 15)))
            for k in range(16)],
            reads=[bStg[wp], bC], writes=[pPeff], waits=[P.bar(f"d_pw{wp}")])
        P.op("vector", lambda v, cj=cj: v.tensor_copy(
            pefft[0:64, 128 * cj:128 * (cj + 1)], B4[0:64, 0:128]),
            reads=[pPeff], writes=[bW])
        P.op("scalar", lambda sc, cj=cj: sc.copy(
            pefft[64:128, 128 * cj:128 * (cj + 1)], B4[0:64, 0:128]),
            reads=[pPeff], writes=[bW])
    P.op("sync", lambda s: s.dma_start(out=bstg[:], in_=qkvb_d),
         writes=[bBstg], dma=True, sem="d_bs")
    for tgt, lo in [(qb, 0), (kb, 64), (vb, 128)]:
        P.op("tensor", lambda t, tgt=tgt, lo=lo: t.transpose(
            B5[0:64, 256:257], bstg[0:1, lo:lo + 64], c_identf[0:1, 0:1]),
            reads=[bBstg, bC], writes=[pSrt], waits=[P.bar("d_bs")])
        P.op("vector", lambda v, tgt=tgt: v.tensor_copy(tgt[:], B5[0:64, 256:257]),
            reads=[pSrt], writes=[bW])
    P.op("vector", lambda v: v.tensor_copy(kvbrow[:], bstg[0:1, 64:192]),
         reads=[bBstg], writes=[bKvb], waits=[P.bar("d_bs")])
    P.op("sync", lambda s: s.dma_start(out=bstg[0:1, 0:E], in_=rb_d),
         writes=[bBstg], dma=True, sem="d_bs")
    P.op("tensor", lambda t: t.matmul(B5[:, 288:288 + E], c_ones1f[:], bstg[0:1, 0:E],
                                      start=True, stop=True),
         reads=[bBstg, bC], writes=[pSrt], waits=[P.bar("d_bs")])
    P.op("vector", lambda v: v.tensor_copy(rbrep[:], B5[:, 288:288 + E]),
         reads=[pSrt], writes=[bW])
    P.op("gpsimd", lambda g: g.dma_start(out=projb_b[:], in_=projb_d),
         writes=[bW], dma=True, sem="d_wg")

    P.op("vector", lambda v: v.memset(xns[0][:], 0.0), writes=[bXns[0]])
    for q in range(SQ):
        P.op("vector", lambda v, q=q: v.memset(
            xns[0][:, CW * q + C:CW * q + C + 1], 65536.0), writes=[bXns[0]])
    for i in range(NT):
        P.op("vector", lambda v, i=i: v.tensor_copy(
            xall[:, CW * i + C:CW * i + C + 2], c_tokhl[:, 2 * i:2 * (i + 1)]),
            reads=[bC], writes=[bXall[i]])

    def emit_zero_chunk(k):
        P.op("sync", lambda s, k=k: s.dma_start(
            out=xnsort_d[SQ * 128 * k:SQ * 128 * (k + 1), :]
                .rearrange("(p n) c -> p n c", p=128),
            in_=xns[0][:].rearrange("p (n c) -> p n c", c=CW)),
            reads=[bXns[0]], dma=True, sem="d_z")

    NZ = SROWS // (128 * SQ)


    def emit_ln_a(i):
        p = i % 4
        xb = bXld[p]
        sB, sT = bStat[p], stat[p]
        P.op("vector", lambda v, p=p, sT=sT: v.reduce_sum(
            sT[:, 0:1], xld[p], axis=AX.X, negate=True),
             reads=[xb], writes=[sB], waits=[P.bar(f"d_x{p}")])
        P.op("vector", lambda v, sT=sT: v.tensor_scalar(sT[:, 1:2], sT[:, 0:1],
                                                        1.0 / C, None, ALU.mult),
             reads=[sB], writes=[sB])
        P.op("vector", lambda v, p=p, sT=sT: v.scalar_tensor_tensor(
            out=lndump[:], in0=xld[p], scalar=sT[:, 1:2], in1=xld[p],
            op0=ALU.add, op1=ALU.mult, accum_out=sT[:, 2:3]),
             reads=[xb, sB], writes=[bLnd, sB])
        P.op("scalar", lambda s, sT=sT: s.activation(
            out=sT[:, 3:4], in_=sT[:, 2:3], func=AF.Sqrt, scale=1.0 / C,
            bias=c_eps[:]),
             reads=[sB, bC], writes=[sB])

    def emit_ln_b(i):
        p = i % 4
        xb = bXld[p]
        sB, sT = bStat[p], stat[p]
        P.op("vector", lambda v, sT=sT: v.reciprocal(sT[:, 4:5], sT[:, 3:4]),
             reads=[sB], writes=[sB])
        P.op("gpsimd", lambda g, i=i, p=p, sT=sT: g.tensor_scalar(
            xall[:, CW * i:CW * i + C], xld[p], sT[:, 1:2], sT[:, 4:5],
            ALU.add, ALU.mult),
             reads=[xb, sB], writes=[bXall[i]])
        if i + 4 < NT:
            x_load(i + 4)

    def emit_front(i):
        p = i % 4
        p2 = i % 2
        px = i % 3
        P.op("sync", lambda s, i=i, px=px: s.dma_start(
            out=xb2[px][:], in_=x_d[128 * i:128 * (i + 1), :]),
            writes=[bXb2[px]], dma=True, sem=f"d_xb{px}")
        TB = B6 if p2 == 0 else B7
        P.grp("tensor", [
            (lambda t, i=i, j=j, TB=TB: t.transpose(
                TB[:, 128 * j:128 * (j + 1)],
                xall[:, CW * i + 128 * j:CW * i + 128 * (j + 1)], c_identbf[:]))
            for j in range(NC8)],
            reads=[bXall[i], bC], writes=[pXnT[p2]])
        P.op("scalar", lambda sc, p2=p2, TB=TB: sc.copy(xnT[p2][:], TB[:]),
             reads=[pXnT[p2]], writes=[bXnT[p2]])
        Bq = B0 if p2 == 0 else B1
        xT = xnT[p2]
        fns = []
        fns += [(lambda t, j=j, Bq=Bq, xT=xT: t.matmul(
            Bq[:, 0:128], wqk[:, 128 * j:128 * (j + 1)],
            xT[:, 128 * j:128 * (j + 1)], start=(j == 0), stop=(j == NC8 - 1)))
            for j in range(NC8)]
        fns += [(lambda t, j=j, Bq=Bq, xT=xT: t.matmul(
            Bq[:, 128:192], xT[:, 128 * j:128 * (j + 1)],
            wqk[:, 128 * j + 64:128 * (j + 1)], start=(j == 0), stop=False))
            for j in range(NC8)]
        fns += [lambda t, Bq=Bq: t.matmul(
            Bq[:, 128:192], c_ones1b[0:1, 0:128], kvbrow[0:1, 0:64],
            start=False, stop=True)]
        fns += [(lambda t, j=j, Bq=Bq, xT=xT: t.matmul(
            Bq[:, 192:256], xT[:, 128 * j:128 * (j + 1)],
            wv[:, 64 * j:64 * (j + 1)], start=(j == 0), stop=False))
            for j in range(NC8)]
        fns += [lambda t, Bq=Bq: t.matmul(
            Bq[:, 192:256], c_ones1b[0:1, 0:128], kvbrow[0:1, 64:128],
            start=False, stop=True)]
        fns += [(lambda t, j=j, Bq=Bq, xT=xT: t.matmul(
            Bq[:, 256:256 + E], xT[:, 128 * j:128 * (j + 1)],
            wrt[:, E * j:E * (j + 1)], start=(j == 0), stop=(j == NC8 - 1)))
            for j in range(NC8)]
        P.grp("tensor", fns, reads=[bW, bXnT[p2], bC, bKvb], writes=[pQkv[p2]])
        P.op("scalar", lambda sc, p=p, Bq=Bq: sc.activation(
            out=qkvt[p][0:64, 0:128], in_=Bq[0:64, 0:128], func=AF.Identity,
            bias=qb[:]),
            reads=[pQkv[p2], bW], writes=[bQkvt[p]])
        P.op("scalar", lambda sc, p=p, Bq=Bq: sc.activation(
            out=qkvt[p][0:64, 128:256], in_=Bq[64:128, 0:128], func=AF.Identity,
            bias=kb[:]),
            reads=[pQkv[p2], bW], writes=[bQkvt[p]])
        P.op("scalar", lambda sc, p=p, Bq=Bq: sc.copy(
            ktv[p][:], Bq[:, 128:256]),
            reads=[pQkv[p2]], writes=[bKtv[p]])
        P.op("vector", lambda v, Bq=Bq: v.tensor_tensor(
            logit[:], Bq[:, 256:256 + E], rbrep[:], ALU.add),
            reads=[pQkv[p2], bW], writes=[bLog])
        P.op("vector", lambda v: v.max(mx8[:], logit[:]), reads=[bLog], writes=[bMx])
        P.op("vector", lambda v: v.max_index(mi8[:], mx8[:], logit[:]),
             reads=[bMx, bLog], writes=[bMi])
        P.op("vector", lambda v: v.tensor_copy(ecol[:], mi8[:, 0:1]),
             reads=[bMi], writes=[bEc])
        P.op("vector", lambda v, i=i: v.tensor_scalar(
            ohall[:, E * i:E * (i + 1)], c_iota32[:], ecol[:], None, ALU.is_equal),
            reads=[bC, bEc], writes=[bOh[i]])
    def emit_cnt(i):
        P.op("vector", lambda v, i=i: v.tensor_copy(
            runrow[0:1, E * i:E * (i + 1)], runcur[:]),
            reads=[bRunc], writes=[bRun[i]])
        P.op("tensor", lambda t, i=i: t.matmul(
            B5[0:1, 128:128 + E], c_onescolb[:], ohall[:, E * i:E * (i + 1)],
            start=True, stop=True),
            reads=[bC, bOh[i]], writes=[pCnt])
        P.op("vector", lambda v: v.tensor_tensor(
            runcur[:], runcur[:], B5[0:1, 128:128 + E], ALU.add),
            reads=[pCnt, bRunc], writes=[bRunc])

    def emit_attn(i):
        p = i % 4
        p2 = i % 2
        BA = B2 if p2 == 0 else B3
        P.op("tensor", lambda t, p=p, BA=BA: t.matmul(
            BA[:, 0:128], qkvt[p][0:64, 128:256], qkvt[p][0:64, 0:128],
            start=True, stop=True),
            reads=[bQkvt[p]], writes=[pA[p2]])
        P.op("vector", lambda v, p=p, BA=BA: v.tensor_tensor(
            A_bf[p2][:], BA[:, 0:128], c_utmask[:], ALU.mult),
            reads=[pA[p2], bC], writes=[bA[p2]])
        ofns = []
        if i > 0:
            ofns.append(lambda t, p=p: t.matmul(
                B4[0:64, 0:128], Sbf[:], qkvt[p][0:64, 0:128],
                start=True, stop=False))
        ofns.append(lambda t, p=p, p2=p2, i=i: t.matmul(
            B4[0:64, 0:128], ktv[p][:, 64:128], A_bf[p2][:],
            start=(i == 0), stop=True))
        P.grp("tensor", ofns, reads=[bSbf, bQkvt[p], bKtv[p], bA[p2]], writes=[pO])
        P.op("scalar", lambda sc, i=i: sc.copy(
            OT[64 * (i % 2):64 * (i % 2) + 64,
               128 * ((i // 2) % 2):128 * ((i // 2) % 2) + 128],
            B4[0:64, 0:128]),
            reads=[pO], writes=[bOT[i % 4]])
        if i < NT - 1:
            P.op("tensor", lambda t, p=p: t.matmul(
                B5[0:64, 0:64], ktv[p][:, 0:64], ktv[p][:, 64:128],
                start=True, stop=True),
                reads=[bKtv[p]], writes=[pS])
            P.op("vector", lambda v: v.tensor_tensor(
                S32[:], S32[:], B5[0:64, 0:64], ALU.add),
                reads=[pS, bS32], writes=[bS32])
            P.op("vector", lambda v: v.tensor_copy(Sbf[:], S32[:]),
                 reads=[bS32], writes=[bSbf])

    def emit_base(i):
        px = i % 3
        p2 = i % 2
        ap = attnp[p2]
        bAp = bAttnp[p2]
        for hf in range(2):
            Bq = B0 if hf == 0 else B1
            P.grp("tensor", [
                (lambda t, i=i, hf=hf, Bq=Bq: t.matmul(
                    Bq[:, 0:512],
                    OT[64 * (i % 2):64 * (i % 2) + 64,
                       128 * ((i // 2) % 2):128 * ((i // 2) % 2) + 128],
                    pefft[64 * (i % 2):64 * (i % 2) + 64,
                          512 * hf:512 * (hf + 1)], start=True, stop=False)),
                (lambda t, hf=hf, Bq=Bq: t.matmul(
                    Bq[:, 0:512], c_ones1b[0:1, 0:128],
                    projb_b[0:1, 512 * hf:512 * (hf + 1)],
                    start=False, stop=True))],
                reads=[bOT[i % 4], bW, bC], writes=[pQkv[hf]])
            P.op("scalar", lambda sc, hf=hf, Bq=Bq, ap=ap: sc.copy(
                ap[:, 512 * hf:512 * (hf + 1)], Bq[:, 0:512]),
                reads=[pQkv[hf]], writes=[bAp])
        P.op("gpsimd", lambda g, px=px, ap=ap: g.tensor_tensor(
            ap[:], ap[:], xb2[px][:], ALU.add),
            reads=[bAp, bXb2[px]], writes=[bAp], waits=[P.bar(f"d_xb{px}")])
        P.op("sync", lambda s, i=i, p2=p2, ap=ap: s.dma_start(
            out=out_d[128 * i:128 * (i + 1), :], in_=ap[:]),
            reads=[bAp], dma=True, sem=f"d_ba{p2}")
        P.settle(bAp, f"d_ba{p2}")

    for k in range(min(4, NT)):
        emit_ln_a(k)
        if k < 2:
            emit_ln_b(k)
    zq = list(range(NZ))
    for k2 in range(0, NT, 2):
        for t in (k2 + 2, k2 + 3):
            if t < NT:
                emit_ln_b(t)
        emit_front(k2)
        if k2 + 1 < NT:
            emit_front(k2 + 1)
        if k2 > 0:
            emit_attn(k2 - 1)
        emit_attn(k2)
        emit_cnt(k2)
        if k2 + 1 < NT:
            emit_cnt(k2 + 1)
        if k2 > 0:
            emit_base(k2 - 1)
        emit_base(k2)
        for t in (k2 + 4, k2 + 5):
            if t < NT:
                emit_ln_a(t)
        nz = (NZ + NT - 1) // NT
        for k in zq[:nz]:
            emit_zero_chunk(k)
        del zq[:nz]
    if NT > 1:
        emit_attn(NT - 1)
        emit_base(NT - 1)
    for k in zq:
        emit_zero_chunk(k)

    P.op("tensor", lambda t: t.transpose(B5[0:E, 192:193], runcur[0:1, 0:E],
                                         c_identf[0:1, 0:1]),
         reads=[bRunc, bC], writes=[pSrt])
    P.op("vector", lambda v: v.tensor_copy(cntcol[:], B5[0:E, 192:193]),
         reads=[pSrt], writes=[bSort])
    P.op("vector", lambda v: v.tensor_scalar(grid[:], c_thrP[:], cntcol[:], None,
                                             ALU.is_lt),
         reads=[bC, bSort], writes=[bSort])
    P.op("vector", lambda v: v.reduce_sum(padded[:], grid[:], axis=AX.X),
         reads=[bSort], writes=[bSort])
    P.op("vector", lambda v: v.tensor_scalar(padded[:], padded[:], float(SEG), None,
                                             ALU.mult),
         reads=[bSort], writes=[bSort])
    P.op("tensor", lambda t: t.matmul(B5[0:1, 224:224 + E], padded[:], c_u32[:],
                                      start=True, stop=True),
         reads=[bSort, bC], writes=[pSrt])
    P.op("vector", lambda v: v.tensor_copy(baser[:], B5[0:1, 224:224 + E]),
         reads=[pSrt], writes=[bSort])
    P.op("tensor", lambda t: t.matmul(B5[0:E, 193:194], c_u32[:], padded[:],
                                      start=True, stop=True),
         reads=[bSort, bC], writes=[pSrt])
    P.op("vector", lambda v: v.tensor_copy(basec[:], B5[0:E, 193:194]),
         reads=[pSrt], writes=[bSort])
    P.op("vector", lambda v: v.tensor_tensor(bpc[:], basec[:], padded[:], ALU.add),
         reads=[bSort], writes=[bSort])
    P.op("vector", lambda v: v.tensor_scalar(segoh[:], c_thrS[:], basec[:], None,
                                             ALU.is_ge),
         reads=[bC, bSort], writes=[bSort])
    P.op("vector", lambda v: v.tensor_scalar(segtmp[:], c_thrS[:], bpc[:], None,
                                             ALU.is_lt),
         reads=[bC, bSort], writes=[bSort])
    P.op("vector", lambda v: v.tensor_tensor(segoh[:], segoh[:], segtmp[:], ALU.mult),
         reads=[bSort], writes=[bSort])
    P.op("tensor", lambda t: t.matmul(B5[0:1, 224 + E:224 + E + NSEG], c_iotaE[:],
                                      segoh[:], start=True, stop=True),
         reads=[bC, bSort], writes=[pSrt])
    P.op("vector", lambda v: v.tensor_copy(seger[:], B5[0:1, 224 + E:224 + E + NSEG]),
         reads=[pSrt], writes=[bSort])
    P.op("tensor", lambda t: t.matmul(B5[0:1, 224 + E:224 + E + NSEG],
                                      c_onescol[0:E, :], segoh[:],
                                      start=True, stop=True),
         reads=[bC, bSort], writes=[pSrt])
    P.op("vector", lambda v: v.tensor_copy(segliv[:], B5[0:1, 224 + E:224 + E + NSEG]),
         reads=[pSrt], writes=[bSort])
    P.op("vector", lambda v: v.tensor_scalar(segdead[:], segliv[:], -OOB, OOB,
                                             ALU.mult, ALU.add),
         reads=[bSort], writes=[bSort])
    P.op("vector", lambda v: v.tensor_scalar(segoff[:], seger[:], 128.0, None,
                                             ALU.mult),
         reads=[bSort], writes=[bSort])
    P.op("vector", lambda v: v.tensor_tensor(segoff[:], segoff[:], segdead[:],
                                             ALU.add),
         reads=[bSort], writes=[bSort])
    P.op("tensor", lambda t: t.matmul(B2[:, 64:64 + NSEG], c_ones1f[:], segoff[:],
                                      start=True, stop=True),
         reads=[bC, bSort], writes=[pRank])
    P.op("vector", lambda v: v.tensor_scalar(idxf[:], B2[:, 64:64 + NSEG],
                                             c_iotaP[:], None, ALU.add),
         reads=[pRank, bC], writes=[bIdxw])
    P.op("vector", lambda v: v.tensor_copy(idxw[:], idxf[:]),
         reads=[bIdxw], writes=[bIdxw])
    P.op("vector", lambda v: v.tensor_tensor(segtmp[0:1, :], seger[:], segdead[:],
                                             ALU.add),
         reads=[bSort], writes=[bSort])
    P.op("tensor", lambda t: t.matmul(B2[0:2, 64:64 + NSEG], c_ones1f[0:1, 0:2],
                                      segtmp[0:1, :], start=True, stop=True),
         reads=[bC, bSort], writes=[pRank])
    P.op("vector", lambda v: v.tensor_copy(idxe[:], B2[0:2, 64:64 + NSEG]),
         reads=[pRank], writes=[bIdxw])

    bndcell = {}

    def mk_bnd(g):
        r = g.alloc_register("gbnd")
        g.reg_mov(r, E * 128 - 1)
        bndcell["v"] = g.snap(r)
        r2 = g.alloc_register("gbnd2")
        g.reg_mov(r2, E - 1)
        bndcell["e"] = g.snap(r2)
        r3 = g.alloc_register("gbnd3")
        g.reg_mov(r3, T - 1)
        bndcell["t"] = g.snap(r3)
        return g.engine_nop()

    P.op("gpsimd", mk_bnd, inc=False)

    def seg_wload(s):
        p = s % 4
        P.op("gpsimd", lambda g, s=s, p=p: g.indirect_dma_start(
            out=wpks[p][:], out_offset=None,
            in_=wpk_d, in_offset=IOA(idxw[:, s:s + 1], 0),
            bounds_check=bndcell["v"], oob_is_err=False),
            reads=[bIdxw], writes=[bWpk[p]], dma=True, sem=f"d_ws{p}")
        P.op("gpsimd", lambda g, s=s, p=p: g.indirect_dma_start(
            out=b2r[p][:], out_offset=None,
            in_=b2r_d, in_offset=IOA(idxe[0:2, s:s + 1], 0),
            bounds_check=bndcell["e"], oob_is_err=False),
            reads=[bIdxw], writes=[bB2r[p]], dma=True, sem=f"d_ws{p}")
        for bb in (bWpk[p], bB2r[p]):
            P.settle(bb, f"d_ws{p}")

    def seg_xload(s):
        p = s % 4
        P.op("sync", lambda sy, s=s, p=p: sy.dma_start(
            out=xns[p][:].rearrange("p (q c) -> p q c", c=CW),
            in_=xnsort_d[SEG * s:SEG * (s + 1), :].rearrange("(q p) c -> p q c", p=128)),
            writes=[bXns[p]], dma=True, sem=f"d_xs{p}",
            waits=[P.bar("d_sct"), P.bar("d_z")])

    def seg_loads(s):
        seg_wload(s)
        seg_xload(s)


    for s0 in range(min(3, NSEG)):
        seg_wload(s0)

    for i in range(NT):
        P.op("vector", lambda v, i=i: v.tensor_tensor(
            brt[:], baser[:], runrow[0:1, E * i:E * (i + 1)], ALU.add),
            reads=[bSort, bRun[i]], writes=[bBrt])
        P.grp("tensor", [
            (lambda t, i=i: t.matmul(
                B2[:, 0:E], c_ustrict[:], ohall[:, E * i:E * (i + 1)],
                start=True, stop=False)),
            (lambda t: t.matmul(B2[:, 0:E], c_ones1f[:], brt[:],
                                start=False, stop=True))],
            reads=[bC, bOh[i], bBrt], writes=[pRank])
        P.op("vector", lambda v, i=i: v.tensor_tensor(
            maskd[:], ohall[:, E * i:E * (i + 1)], B2[:, 0:E], ALU.mult),
            reads=[bOh[i], pRank], writes=[bMaskd])
        P.op("vector", lambda v: v.reduce_sum(rankf[:], maskd[:], axis=AX.X),
             reads=[bMaskd], writes=[bRankf])
        P.op("vector", lambda v, i=i: v.tensor_copy(rank32[:, i:i + 1], rankf[:]),
             reads=[bRankf], writes=[bRk[i]])
    for i in range(NT):
        P.op("gpsimd", lambda g, i=i: g.indirect_dma_start(
            out=xnsort_d, out_offset=IOA(rank32[:, i:i + 1], 0),
            in_=xall[:, CW * i:CW * (i + 1)], in_offset=None),
            reads=[bRk[i], bXall[i]], dma=True, sem="d_sct",
            waits=[P.bar("d_z")])
    for s0 in range(min(3, NSEG)):
        seg_xload(s0)
    def seg_tr_q(s, q):
        p = s % 4
        p2 = s % 2
        TB = B6 if p2 == 0 else B7
        if q == 0:
            P.op("vector", lambda v, p=p: v.tensor_copy(
                bcol[p][:], wpks[p][:, 4 * C:4 * C + 2]),
                reads=[bWpk[p]], writes=[bBcol[p]])
        if q >= SQ:
            return
        P.grp("tensor", [
            (lambda t, p=p, q=q, j=j, TB=TB: t.transpose(
                TB[:, 128 * j:128 * (j + 1)],
                xns[p][:, CW * q + 128 * j:CW * q + 128 * (j + 1)],
                c_identbf[:])) for j in range(NC8)],
            reads=[bXns[p], bC], writes=[pXnT[p2]],
            waits=[P.bar(f"d_xs{p}")] if q == 0 else ())
        P.op("vector", lambda v, p2=p2, q=q, TB=TB: v.tensor_copy(
            xnsT[p2][:, C * q:C * (q + 1)], TB[:]),
            reads=[pXnT[p2]], writes=[bXnsT[p2]])
        for cc in range(2):
            P.op("vector", lambda v, p=p, q=q, cc=cc: v.tensor_scalar(
                tokf2[p][:, 2 * q + cc:2 * q + cc + 1],
                xns[p][:, CW * q + C + cc:CW * q + C + cc + 1],
                256.0, None, ALU.mult),
                reads=[bXns[p]], writes=[bTk[p]])
        P.op("vector", lambda v, p=p, q=q: v.tensor_tensor(
            tokf[p][:, q:q + 1], tokf2[p][:, 2 * q:2 * q + 1],
            tokf2[p][:, 2 * q + 1:2 * q + 2], ALU.add),
            reads=[bTk[p]], writes=[bTk[p]])
        if q == SQ - 1:
            P.op("vector", lambda v, p=p: v.tensor_copy(tokid[p][:], tokf[p][:]),
                 reads=[bTk[p]], writes=[bTk[p]])

    def seg_tr(s):
        for q in range(SQ):
            seg_tr_q(s, q)

    def seg_h(s):
        p = s % 4
        p2 = s % 2
        BH = B0 if p2 == 0 else B1
        xnsT4 = xnsT[p2][:].rearrange("p (q j t) -> p q j t", q=SQ, j=NC8)
        for hh in range(2):
            P.grp("tensor", [
                (lambda t, p=p, p2=p2, hh=hh, j=j, BH=BH, xnsT4=xnsT4: t.matmul(
                    BH[:, SEG * hh:SEG * (hh + 1)],
                    wpks[p][:, H * j + 128 * hh:H * j + 128 * (hh + 1)],
                    xnsT4[:, :, j, :], start=(j == 0), stop=(j == NC8 - 1)))
                for j in range(NC8)],
                reads=[bWpk[p], bXnsT[p2]],
                writes=[pH[p2]] + ([pQkv[s]] if s < 2 and hh == 0 else []),
                waits=[P.bar(f"d_ws{p}")] if hh == 0 else ())
            P.op("scalar", lambda sc, p=p, p2=p2, hh=hh, BH=BH: sc.activation(
                out=hfm[p2][:, SEG * hh:SEG * (hh + 1)],
                in_=BH[:, SEG * hh:SEG * (hh + 1)], func=AF.Relu,
                bias=bcol[p][:, hh:hh + 1]),
                reads=[pH[p2], bBcol[p]], writes=[bHfm[p2]])

    def seg_y(s, interleave=None):
        p = s % 4
        p2 = s % 2
        for q in range(SQ):
            if interleave is not None:
                interleave(q)
            for cb in range(2):
                yb = (q * 2 + cb) % 2
                BY = B2 if yb == 0 else B3
                yfns = [(lambda t, p=p, p2=p2, q=q, cb=cb, j=j, BY=BY: t.matmul(
                    BY[:, 0:512],
                    hfm[p2][:, SEG * j + 128 * q:SEG * j + 128 * (q + 1)],
                    wpks[p][:, 2 * C + C * j + 512 * cb:2 * C + C * j + 512 * (cb + 1)],
                    start=(j == 0), stop=False)) for j in range(2)]
                yfns.append(lambda t, p=p, cb=cb, BY=BY: t.matmul(
                    BY[:, 0:512], c_ones1b[0:1, 0:128],
                    b2r[p][0:1, 512 * cb:512 * (cb + 1)], start=False, stop=True))
                P.grp("tensor", yfns,
                      reads=[bWpk[p], bHfm[p2], bB2r[p], bC], writes=[pY[yb]])
                if (q * 2 + cb) % 2 == 0:
                    P.op("scalar", lambda sc, p2=p2, q=q, cb=cb, BY=BY: sc.activation(
                        out=ytm[p2][:, C * q + 512 * cb:C * q + 512 * (cb + 1)],
                        in_=BY[:, 0:512], func=AF.Relu),
                        reads=[pY[yb]], writes=[bYtm[p2]])
                else:
                    P.op("vector", lambda v, p2=p2, q=q, cb=cb, BY=BY: v.tensor_scalar(
                        ytm[p2][:, C * q + 512 * cb:C * q + 512 * (cb + 1)],
                        BY[:, 0:512], 0.0, None, ALU.max),
                        reads=[pY[yb]], writes=[bYtm[p2]])
        for q in range(SQ):
            P.op("gpsimd", lambda g, p=p, p2=p2, q=q: g.indirect_dma_start(
                out=ydst_d, out_offset=IOA(tokid[p][:, q:q + 1], 0),
                in_=ytm[p2][:, C * q:C * (q + 1)], in_offset=None,
                bounds_check=bndcell["t"], oob_is_err=False),
                reads=[bTk[p], bYtm[p2]], dma=True, sem=f"d_yo{p2}",
                waits=[P.bar("d_yo0"), P.bar("d_yo1")])
        P.settle(bYtm[p2], f"d_yo{p2}")
        P.settle(bTk[p], f"d_yo{p2}")

    seg_tr(0)
    for s in range(NSEG):
        seg_h(s)
        if s + 1 < NSEG:
            if s > 0:
                seg_y(s - 1, interleave=lambda k, s=s: seg_tr_q(s + 1, k))
            else:
                seg_tr(1)
        elif s > 0:
            seg_y(s - 1)
        if s + 3 < NSEG:
            seg_loads(s + 3)
    seg_y(NSEG - 1)

    yld = [outt[0], outt[1], scr1]
    bYld = [bLnd, bStg[1], bStg[0]]

    def p3_loads(i):
        px = i % 3
        P.op("sync", lambda s, i=i, px=px: s.dma_start(
            out=xb2[px][:], in_=out_d[128 * i:128 * (i + 1), :]),
            writes=[bXb2[px]], dma=True, sem=f"d_xb{px}",
            waits=[P.bar("d_ba0"), P.bar("d_ba1")])
        P.op("sync", lambda s, i=i, px=px: s.dma_start(
            out=yld[px][:], in_=ydst_d[128 * i:128 * (i + 1), :]),
            writes=[bYld[px]], dma=True, sem=f"d_yl{px}",
            waits=[P.bar("d_yo0"), P.bar("d_yo1")])

    for i0 in range(min(3, NT)):
        p3_loads(i0)
    for i in range(NT):
        px = i % 3
        po = i % 2
        P.op("vector", lambda v, px=px: v.tensor_tensor(
            xb2[px][:, 0:512], xb2[px][:, 0:512], yld[px][:, 0:512], ALU.add),
            reads=[bXb2[px], bYld[px]], writes=[bXb2[px]],
            waits=[P.bar(f"d_xb{px}"), P.bar(f"d_yl{px}")])
        P.op("gpsimd", lambda g, px=px: g.tensor_tensor(
            xb2[px][:, 512:1024], xb2[px][:, 512:1024], yld[px][:, 512:1024],
            ALU.add),
            reads=[bXb2[px], bYld[px]], writes=[bXb2[px]],
            waits=[P.bar(f"d_xb{px}"), P.bar(f"d_yl{px}")])
        P.op("sync", lambda s, i=i, px=px: s.dma_start(
            out=out_d[128 * i:128 * (i + 1), :], in_=xb2[px][:]),
            reads=[bXb2[px]], dma=True, sem=f"d_po{po}")
        if i + 3 < NT:
            p3_loads(i + 3)

    P.op("vector", lambda v: v.engine_nop(),
         waits=[P.bar("d_po0"), P.bar("d_po1")], inc=False)

    P.build(st)
    st.close()
    return nc


def pack_ffn(w1, b1, w2, b2):
    E, C, H = 32, 1024, 256
    NC8 = C // 128
    w1 = np.asarray(w1, dtype=np.float32)
    b1 = np.asarray(b1, dtype=np.float32)
    w2 = np.asarray(w2, dtype=np.float32)
    b2 = np.asarray(b2, dtype=np.float32)
    w1p = w1.reshape(E, NC8, 128, H).transpose(0, 2, 1, 3).reshape(E, 128, NC8 * H)
    w2p = w2.reshape(E, 2, 128, C).transpose(0, 2, 1, 3).reshape(E, 128, 2 * C)
    b1p = b1.reshape(E, 2, 128).transpose(0, 2, 1)
    b2p = b2.reshape(E, NC8, 128).transpose(0, 2, 1)
    wpk = np.concatenate([w1p, w2p, b1p, b2p], axis=2).reshape(E * 128, 4 * C + 2 + NC8)
    return {"wpk": np.ascontiguousarray(wpk), "b2row": np.ascontiguousarray(b2)}


def make_consts(T=4096, SEG=256):
    E = 32
    assert T % 128 == 0
    NSEG = T // SEG + E
    NPT = T // SEG
    bf16 = ml_dtypes.bfloat16
    ident = np.eye(128, dtype=np.float32)
    k = np.arange(128)
    utmask = (k[:, None] <= k[None, :]).astype(np.float32)
    ustrict = (k[:, None] < k[None, :]).astype(np.float32)
    e = np.arange(E)
    u32 = (e[:, None] < e[None, :]).astype(np.float32)
    return {
        "identbf": ident.astype(bf16),
        "identf": ident,
        "utmask": utmask.astype(bf16),
        "iota32": np.tile(np.arange(E, dtype=np.float32), (128, 1)),
        "iotaE": e.astype(np.float32).reshape(E, 1),
        "iotaP": np.arange(128, dtype=np.float32).reshape(128, 1),
        "u32": u32,
        "ustrict": ustrict.astype(bf16),
        "thrP": np.tile((np.arange(NPT) * SEG).astype(np.float32), (E, 1)),
        "thrS": np.tile((np.arange(NSEG) * SEG).astype(np.float32), (E, 1)),
        "tokhl": np.stack(
            [np.arange(T).reshape(-1, 128)[t // 2][:] // 256 if t % 2 == 0
             else np.arange(T).reshape(-1, 128)[t // 2][:] % 256 / 256.0
             for t in range(2 * (T // 128))], axis=1).astype(bf16),
    }


def kernel(L0_attn_ids, L0_attn_qkv_w, L0_attn_qkv_b, L0_attn_proj_w, L0_attn_proj_b,
           L0_ffn_router_w, L0_ffn_router_b, L0_ffn_weights1, L0_ffn_bias1,
           L0_ffn_weights2, L0_ffn_bias2):
    T, C, E, H = 4096, 1024, 32, 256
    SEG = 256
    x = np.ascontiguousarray(np.asarray(L0_attn_ids, dtype=np.float32))
    weights = {
        "qkv_w": np.ascontiguousarray(np.asarray(L0_attn_qkv_w, dtype=np.float32)),
        "qkv_b": np.asarray(L0_attn_qkv_b, dtype=np.float32).reshape(1, 192),
        "proj_w": np.ascontiguousarray(np.asarray(L0_attn_proj_w, dtype=np.float32)),
        "proj_b": np.asarray(L0_attn_proj_b, dtype=np.float32).reshape(1, C),
        "router_w": np.ascontiguousarray(np.asarray(L0_ffn_router_w, dtype=np.float32)),
        "router_b": np.asarray(L0_ffn_router_b, dtype=np.float32).reshape(1, E),
    }
    weights.update(pack_ffn(L0_ffn_weights1, L0_ffn_bias1,
                            L0_ffn_weights2, L0_ffn_bias2))
    weights.update(make_consts(T, SEG))

    nc = bass.Bass("TRN2", target_bir_lowering=False)
    build_program(nc, T=T, SEG=SEG)

    in_maps = []
    for b in range(8):
        m = {"x": x[b]}
        m.update(weights)
        in_maps.append(m)
    res = run_bass_kernel_spmd(nc, in_maps, core_ids=list(range(8)))
    return np.stack([r["out"] for r in res.results], axis=0)


# version 37
# speedup vs baseline: 1.3276x; 1.3276x over previous
import contextlib

import numpy as np
import ml_dtypes

import concourse.bass as bass
import concourse.mybir as mybir
from concourse.bass_utils import run_bass_kernel_spmd

F32 = mybir.dt.float32
BF16 = mybir.dt.bfloat16
I32 = mybir.dt.int32
U32 = mybir.dt.uint32
AF = mybir.ActivationFunctionType
ALU = mybir.AluOpType
AX = mybir.AxisListType
DS = bass.ds
IOA = bass.IndirectOffsetOnAxis

ENGINES = ("sync", "gpsimd", "tensor", "vector", "scalar")


class Buf:

    __slots__ = ("name", "w", "r")

    def __init__(self, name):
        self.name = name
        self.w = {}
        self.r = {}


class Prog:

    def __init__(self, nc):
        self.nc = nc
        self.items = {e: [] for e in ENGINES}
        self.counts = {}
        self.qwm = {e: {} for e in ENGINES}

    def bar(self, sem):
        return (sem, self.counts.get(sem, 0))

    def _mkwaits(self, eng, reads, writes, waits):
        w = {}

        def add(t):
            if t is None:
                return
            s, c = t
            if w.get(s, 0) < c:
                w[s] = c

        for t in waits:
            add(t)
        for b in reads:
            for it in b.w.items():
                add(it)
        for b in writes:
            for it in b.w.items():
                add(it)
            for it in b.r.items():
                add(it)
        wm = self.qwm[eng]
        for s in list(w):
            c = w[s]
            if s.startswith("d_"):
                if c <= wm.get(s, 0):
                    del w[s]
                    continue
                assert c == self.counts.get(s, 0), (
                    f"non-barrier wait on DMA sem {s}: {c} != {self.counts.get(s, 0)}"
                )
            if wm.get(s, 0) < c:
                wm[s] = c
        return w

    def _record(self, reads, writes, tok):
        for b in reads:
            if b.r.get(tok[0], 0) < tok[1]:
                b.r[tok[0]] = tok[1]
        for b in writes:
            if b.w.get(tok[0], 0) < tok[1]:
                b.w[tok[0]] = tok[1]
            b.r = {}

    def op(self, eng, fn, reads=(), writes=(), waits=(), dma=False, sem=None, inc=True):
        sem = sem or (("d_" if dma else "s_") + eng)
        amt = 16 if dma else 1
        w = self._mkwaits(eng, reads, writes, waits)
        tok = None
        if inc:
            self.counts[sem] = self.counts.get(sem, 0) + amt
            tok = (sem, self.counts[sem])
        if reads or writes:
            assert tok is not None
        self._record(reads, writes, tok) if tok else None
        self.items[eng].append((fn, w, sem if inc else None, amt))
        return tok

    def grp(self, eng, fns, reads=(), writes=(), waits=()):
        sem = "s_" + eng
        w = self._mkwaits(eng, reads, writes, waits)
        for k, fn in enumerate(fns):
            if k == len(fns) - 1:
                self.counts[sem] = self.counts.get(sem, 0) + 1
                tok = (sem, self.counts[sem])
                self.items[eng].append((fn, w if len(fns) == 1 else {}, sem, 1))
            else:
                self.items[eng].append((fn, w if k == 0 else {}, None, 1))
        self._record(reads, writes, tok)
        return tok

    def settle(self, buf, sem):
        buf.w = {sem: self.counts.get(sem, 0)}

    def build(self, stack):
        nc = self.nc
        used = set()
        for e in ENGINES:
            for _, w, s, _ in self.items[e]:
                if s:
                    used.add(s)
                used.update(w.keys())
        sems = {s: stack.enter_context(nc.semaphore(s)) for s in sorted(used)}
        block = stack.enter_context(nc.Block())

        def mk(eng_name):
            items = self.items[eng_name]

            def section(eng):
                seen = {}
                for fn, waits, inc_sem, amt in items:
                    for s, c in waits.items():
                        if seen.get(s, 0) < c:
                            eng.wait_ge(sems[s], c)
                            seen[s] = c
                    inst = fn(eng)
                    if inc_sem:
                        inst.then_inc(sems[inc_sem], amt)

            return section

        for e in ENGINES:
            if self.items[e]:
                getattr(block, e)(mk(e))


def build_program(nc, T=4096, SEG=256):
    C, HS, E, H = 1024, 64, 32, 256
    NT = T // 128
    NC8 = C // 128
    NSEG = T // SEG + E
    SQ = SEG // 128
    SROWS = NSEG * SEG
    NPT = T // SEG
    OOB = float(1 << 25)

    dt = nc.dram_tensor
    x_d = dt("x", [T, C], F32, kind="ExternalInput").ap()
    qkvw_d = dt("qkv_w", [192, C], F32, kind="ExternalInput").ap()
    qkvb_d = dt("qkv_b", [1, 192], F32, kind="ExternalInput").ap()
    projw_d = dt("proj_w", [C, C], F32, kind="ExternalInput").ap()
    projb_d = dt("proj_b", [1, C], F32, kind="ExternalInput").ap()
    rw_d = dt("router_w", [E, C], F32, kind="ExternalInput").ap()
    rb_d = dt("router_b", [1, E], F32, kind="ExternalInput").ap()
    WPC = 4 * C + 2 + NC8
    wpk_d = dt("wpk", [E * 128, WPC], F32, kind="ExternalInput").ap()
    b2r_d = dt("b2row", [E, C], F32, kind="ExternalInput").ap()
    identbf_d = dt("identbf", [128, 128], BF16, kind="ExternalInput").ap()
    identf_d = dt("identf", [128, 128], F32, kind="ExternalInput").ap()
    utmask_d = dt("utmask", [128, 128], BF16, kind="ExternalInput").ap()
    iota32_d = dt("iota32", [128, E], F32, kind="ExternalInput").ap()
    iotaE_d = dt("iotaE", [E, 1], F32, kind="ExternalInput").ap()
    iotaP_d = dt("iotaP", [128, 1], F32, kind="ExternalInput").ap()
    u32_d = dt("u32", [E, E], F32, kind="ExternalInput").ap()
    ustrict_d = dt("ustrict", [128, 128], BF16, kind="ExternalInput").ap()
    thrP_d = dt("thrP", [E, NPT], F32, kind="ExternalInput").ap()
    thrS_d = dt("thrS", [E, NSEG], F32, kind="ExternalInput").ap()
    out_d = dt("out", [T, C], F32, kind="ExternalOutput").ap()
    xnsort_d = dt("xnsort", [SROWS, C], BF16, kind="Internal").ap()
    ysort_d = dt("ysort", [SROWS, C], BF16, kind="Internal").ap()

    st = contextlib.ExitStack()
    sb = lambda name, shape, dty: st.enter_context(nc.sbuf_tensor(name, shape, dty))
    ps = lambda name, shape, dty: st.enter_context(nc.psum_tensor(name, shape, dty))

    c_identbf = sb("c_identbf", [128, 128], BF16)
    c_identf = sb("c_identf", [128, 128], F32)
    c_utmask = sb("c_utmask", [128, 128], BF16)
    c_iota32 = sb("c_iota32", [128, E], F32)
    c_iotaE = sb("c_iotaE", [E, 1], F32)
    c_u32 = sb("c_u32", [E, E], F32)
    c_ustrict = sb("c_ustrict", [128, 128], BF16)
    c_thrP = sb("c_thrP", [E, NPT], F32)
    c_thrS = sb("c_thrS", [E, NSEG], F32)
    c_onescol = sb("c_onescol", [128, 1], F32)
    c_onescolb = sb("c_onescolb", [128, 1], BF16)
    c_ones1f = sb("c_ones1f", [1, 128], F32)
    c_ones1b = sb("c_ones1b", [1, max(SEG, 512)], BF16)
    c_eps = sb("c_eps", [128, 1], F32)

    wqk = sb("wqk", [128, NC8 * 128], BF16)
    wv = sb("wv", [128, NC8 * 64], BF16)
    wrt = sb("wrt", [128, NC8 * E], BF16)
    pefft = sb("pefft", [128, C], BF16)
    projb_b = sb("projb_b", [1, C], BF16)
    qb = sb("qb", [64, 1], F32)
    kb = sb("kb", [64, 1], F32)
    vb = sb("vb", [64, 1], F32)
    rbrep = sb("rbrep", [128, E], F32)

    bstg = sb("bstg", [1, 192], F32)

    xall = sb("xall", [128, NT * C], BF16)
    OT = sb("OTbuf", [128, T // 2], BF16)
    ohall = sb("ohall", [128, NT * E], BF16)
    rank32 = sb("rank32", [128, NT], I32)
    runrow = sb("runrow", [1, NT * E], F32)
    runcur = sb("runcur", [1, E], F32)
    xld = [sb(f"xld{p}", [128, C], F32) for p in range(4)]
    stat = [sb(f"stat{p}", [128, 8], F32) for p in range(4)]
    xnT = [sb(f"xnT{p}", [128, C], BF16) for p in range(2)]
    qkvt = [sb(f"qkvt{p}", [64, 256], BF16) for p in range(4)]
    kvbrow = sb("kvbrow", [1, 128], BF16)
    ktv = [sb(f"ktv{p}", [128, 128], BF16) for p in range(4)]
    A_bf = [sb(f"A_bf{p}", [128, 128], BF16) for p in range(2)]
    S32 = sb("S32", [64, 64], F32)
    Sbf = sb("Sbf", [64, 64], BF16)
    logit = sb("logit", [128, E], F32)
    mx8 = sb("mx8", [128, 8], F32)
    mi8 = sb("mi8", [128, 8], U32)
    ecol = sb("ecol", [128, 1], F32)

    cntcol = sb("cntcol", [E, 1], F32)
    grid = sb("grid", [E, NPT], F32)
    padded = sb("padded", [E, 1], F32)
    baser = sb("baser", [1, E], F32)
    basec = sb("basec", [E, 1], F32)
    bpc = sb("bpc", [E, 1], F32)
    segoh = sb("segoh", [E, NSEG], F32)
    segtmp = sb("segtmp", [E, NSEG], F32)
    seger = sb("seger", [1, NSEG], F32)
    segliv = sb("segliv", [1, NSEG], F32)
    segdead = sb("segdead", [1, NSEG], F32)
    segoff = sb("segoff", [1, NSEG], F32)
    idxf = sb("idxf", [128, NSEG], F32)
    idxw = sb("idxw", [128, NSEG], I32)
    c_iotaP = sb("c_iotaP", [128, 1], F32)
    brt = sb("brt", [1, E], F32)
    maskd = sb("maskd", [128, E], F32)
    rankf = sb("rankf", [128, 1], F32)

    wpks = [sb(f"wpks{p}", [128, WPC], BF16) for p in range(4)]
    bcol = [sb(f"bcol{p}", [128, 2], F32) for p in range(4)]
    xns = [sb(f"xns{p}", [128, SQ * C], BF16) for p in range(4)]
    xnsT = [sb(f"xnsT{p}", [128, SQ * C], BF16) for p in range(2)]
    hfm = [sb(f"hfm{p}", [128, 2 * SEG], BF16) for p in range(2)]
    ytm = [sb(f"ytm{p}", [128, SQ * C], BF16) for p in range(2)]
    b2r = [sb(f"b2r{p}", [2, C], BF16) for p in range(4)]
    idxe = sb("idxe", [2, NSEG], I32)
    yt = [sb(f"yt{p}", [128, C], BF16) for p in range(3)]
    scr1 = sb("scr1", [128, C], F32)
    outt = [sb(f"outt{p}", [128, C], F32) for p in range(2)]
    wstg = [scr1, outt[1]]
    lndump = outt[0]
    wstgb = yt[0]

    B0 = ps("B0", [128, 512], F32)
    B1 = ps("B1", [128, 512], F32)
    B2 = ps("B2", [128, 512], F32)
    B3 = ps("B3", [128, 512], F32)
    B4 = ps("B4", [128, 512], F32)
    B5 = ps("B5", [128, 512], F32)
    B6 = ps("B6", [128, 1024], BF16)
    B7 = ps("B7", [128, 1024], BF16)

    P = Prog(nc)
    nb = lambda n: Buf(n)

    bC = nb("consts")
    bW = nb("wpro")
    bStg = [nb("wstg0"), nb("wstg1")]
    bLnd = nb("lndump")
    bStgb = nb("wstgb")
    bBstg = nb("bstg")
    bXall = [nb(f"xall{i}") for i in range(NT)]
    bOT = [nb(f"OT{i}") for i in range(NT)]
    bOh = [nb(f"oh{i}") for i in range(NT)]
    bRk = [nb(f"rk{i}") for i in range(NT)]
    bRun = [nb(f"run{i}") for i in range(NT)]
    bRunc = nb("runcur")
    bXld = [nb(f"xld{p}") for p in range(4)]
    bStat = [nb(f"stat{p}") for p in range(4)]
    bXnT = [nb(f"xnT{p}") for p in range(2)]
    bKvb = nb("kvbrow")
    bQkvt = [nb(f"qkvt{p}") for p in range(4)]
    bKtv = [nb(f"ktv{p}") for p in range(4)]
    bA = [nb(f"A_bf{p}") for p in range(2)]
    bS32 = nb("S32")
    bSbf = nb("Sbf")
    bLog = nb("logit")
    bMx = nb("mx8")
    bMi = nb("mi8")
    bEc = nb("ecol")
    bSort = nb("sorttabs")
    bIdxw = nb("idxw")
    bBrt = nb("brt")
    bMaskd = nb("maskd")
    bRankf = nb("rankf")
    bWpk = [nb(f"wpks{p}") for p in range(4)]
    bBcol = [nb(f"bcol{p}") for p in range(4)]
    bXns = [nb(f"xns{p}") for p in range(4)]
    bXnsT = [nb(f"xnsT{p}") for p in range(2)]
    bHfm = [nb(f"hfm{p}") for p in range(2)]
    bYtm = [nb(f"ytm{p}") for p in range(2)]
    bB2r = [nb(f"b2r{p}") for p in range(4)]
    bYt = [nb(f"yt{p}") for p in range(3)]
    bOutt = [nb(f"outt{p}") for p in range(2)]
    pQkv = [nb(f"pQkv{p}") for p in range(2)]
    pA = [nb(f"pA{p}") for p in range(2)]
    pO = nb("pO")
    pS = nb("pS")
    pCnt = nb("pCnt")
    pSrt = nb("pSrt")
    pRank = nb("pRank")
    pIdx = nb("pIdx")
    pPeff = nb("pPeff")
    pXnT = [nb(f"pXnT{p}") for p in range(2)]
    pH = [nb(f"pH{p}") for p in range(2)]
    pY = [nb(f"pY{p}") for p in range(2)]
    pOut = [[nb(f"pOut{par}{h}") for h in range(2)] for par in range(2)]

    for tgt, src in [
        (c_identbf, identbf_d), (c_identf, identf_d), (c_utmask, utmask_d),
        (c_iota32, iota32_d), (c_iotaE, iotaE_d), (c_u32, u32_d), (c_iotaP, iotaP_d),
        (c_ustrict, ustrict_d), (c_thrP, thrP_d), (c_thrS, thrS_d),
    ]:
        P.op("sync", lambda s, a=tgt, b=src: s.dma_start(out=a[:], in_=b),
             writes=[bC], dma=True, sem="d_c")
    P.op("vector", lambda v: v.memset(c_onescol[:], 1.0), writes=[bC])
    P.op("vector", lambda v: v.memset(c_onescolb[:], 1.0), writes=[bC])
    P.op("vector", lambda v: v.memset(c_ones1f[:], 1.0), writes=[bC])
    P.op("vector", lambda v: v.memset(c_ones1b[:], 1.0), writes=[bC])
    P.op("vector", lambda v: v.memset(c_eps[:], 1e-5), writes=[bC])
    P.op("vector", lambda v: v.memset(runcur[:], 0.0), writes=[bRunc])
    P.op("vector", lambda v: v.memset(S32[:], 0.0), writes=[bS32])
    P.op("vector", lambda v: v.memset(Sbf[:], 0.0), writes=[bSbf])

    def x_load(i):
        p = i % 4
        P.op("sync", lambda s, i=i, p=p: s.dma_start(
            out=xld[p][:], in_=x_d[128 * i:128 * (i + 1), :]),
            writes=[bXld[p]], dma=True, sem=f"d_x{p}")
    for k in range(min(4, NT)):
        x_load(k)

    P.op("gpsimd", lambda g: g.dma_start(out=wstgb[:], in_=qkvw_d[0:128, :]),
         writes=[bStgb], dma=True, sem="d_wg")
    P.grp("tensor", [
        (lambda t, j=j: t.transpose(
            B6[:, 128 * j:128 * (j + 1)], wstgb[:, 128 * j:128 * (j + 1)],
            c_identbf[:])) for j in range(NC8)],
        reads=[bStgb, bC], writes=[pXnT[0]])
    P.op("vector", lambda v: v.tensor_copy(wqk[:], B6[:]), reads=[pXnT[0]], writes=[bW])
    P.op("gpsimd", lambda g: g.dma_start(out=wstgb[0:64, :], in_=qkvw_d[128:192, :]),
         writes=[bStgb], dma=True, sem="d_wg")
    P.grp("tensor", [
        (lambda t, j=j: t.transpose(
            B6[:, 64 * j:64 * (j + 1)], wstgb[0:64, 128 * j:128 * (j + 1)],
            c_identbf[0:64, 0:64])) for j in range(NC8)],
        reads=[bStgb, bC], writes=[pXnT[0]])
    P.op("vector", lambda v: v.tensor_copy(wv[:], B6[:, 0:NC8 * 64]),
         reads=[pXnT[0]], writes=[bW])
    P.op("gpsimd", lambda g: g.dma_start(out=wstgb[0:E, :], in_=rw_d),
         writes=[bStgb], dma=True, sem="d_wg")
    P.grp("tensor", [
        (lambda t, j=j: t.transpose(
            B6[:, E * j:E * (j + 1)], wstgb[0:E, 128 * j:128 * (j + 1)],
            c_identbf[0:E, 0:E])) for j in range(NC8)],
        reads=[bStgb, bC], writes=[pXnT[0]])
    P.op("vector", lambda v: v.tensor_copy(wrt[:], B6[:, 0:NC8 * E]),
         reads=[pXnT[0]], writes=[bW])
    for cj in range(NC8):
        wp = cj % 2
        P.op("sync", lambda s, cj=cj, wp=wp: s.dma_start(
            out=wstg[wp][:], in_=projw_d[128 * cj:128 * (cj + 1), :]),
            writes=[bStg[wp]], dma=True, sem=f"d_pw{wp}")
        P.grp("tensor", [
            (lambda t, wp=wp, k=k: t.matmul(
                B4[0:64, 0:128], wstg[wp][:, 64 * k:64 * (k + 1)], c_identf[:],
                is_transpose=True, start=(k == 0), stop=(k == 15)))
            for k in range(16)],
            reads=[bStg[wp], bC], writes=[pPeff], waits=[P.bar(f"d_pw{wp}")])
        P.op("vector", lambda v, cj=cj: v.tensor_copy(
            pefft[0:64, 128 * cj:128 * (cj + 1)], B4[0:64, 0:128]),
            reads=[pPeff], writes=[bW])
        P.op("scalar", lambda sc, cj=cj: sc.copy(
            pefft[64:128, 128 * cj:128 * (cj + 1)], B4[0:64, 0:128]),
            reads=[pPeff], writes=[bW])
    P.op("sync", lambda s: s.dma_start(out=bstg[:], in_=qkvb_d),
         writes=[bBstg], dma=True, sem="d_bs")
    for tgt, lo in [(qb, 0), (kb, 64), (vb, 128)]:
        P.op("tensor", lambda t, tgt=tgt, lo=lo: t.transpose(
            B5[0:64, 256:257], bstg[0:1, lo:lo + 64], c_identf[0:1, 0:1]),
            reads=[bBstg, bC], writes=[pSrt], waits=[P.bar("d_bs")])
        P.op("vector", lambda v, tgt=tgt: v.tensor_copy(tgt[:], B5[0:64, 256:257]),
            reads=[pSrt], writes=[bW])
    P.op("vector", lambda v: v.tensor_copy(kvbrow[:], bstg[0:1, 64:192]),
         reads=[bBstg], writes=[bKvb], waits=[P.bar("d_bs")])
    P.op("sync", lambda s: s.dma_start(out=bstg[0:1, 0:E], in_=rb_d),
         writes=[bBstg], dma=True, sem="d_bs")
    P.op("tensor", lambda t: t.matmul(B5[:, 288:288 + E], c_ones1f[:], bstg[0:1, 0:E],
                                      start=True, stop=True),
         reads=[bBstg, bC], writes=[pSrt], waits=[P.bar("d_bs")])
    P.op("vector", lambda v: v.tensor_copy(rbrep[:], B5[:, 288:288 + E]),
         reads=[pSrt], writes=[bW])
    P.op("gpsimd", lambda g: g.dma_start(out=projb_b[:], in_=projb_d),
         writes=[bW], dma=True, sem="d_wg")

    P.op("vector", lambda v: v.memset(xns[0][:], 0.0), writes=[bXns[0]])

    def emit_zero_chunk(k):
        P.op("sync", lambda s, k=k: s.dma_start(
            out=xnsort_d[SQ * 128 * k:SQ * 128 * (k + 1), :]
                .rearrange("(p n) c -> p n c", p=128),
            in_=xns[0][:].rearrange("p (n c) -> p n c", c=C)),
            reads=[bXns[0]], dma=True, sem="d_z")

    NZ = SROWS // (128 * SQ)


    def emit_ln_a(i):
        p = i % 4
        xb = bXld[p]
        sB, sT = bStat[p], stat[p]
        P.op("vector", lambda v, p=p, sT=sT: v.reduce_sum(
            sT[:, 0:1], xld[p][:], axis=AX.X, negate=True),
             reads=[xb], writes=[sB], waits=[P.bar(f"d_x{p}")])
        P.op("vector", lambda v, sT=sT: v.tensor_scalar(sT[:, 1:2], sT[:, 0:1],
                                                        1.0 / C, None, ALU.mult),
             reads=[sB], writes=[sB])
        P.op("vector", lambda v, p=p, sT=sT: v.scalar_tensor_tensor(
            out=lndump[:], in0=xld[p][:], scalar=sT[:, 1:2], in1=xld[p][:],
            op0=ALU.add, op1=ALU.mult, accum_out=sT[:, 2:3]),
             reads=[xb, sB], writes=[bLnd, sB])
        P.op("scalar", lambda s, sT=sT: s.activation(
            out=sT[:, 3:4], in_=sT[:, 2:3], func=AF.Sqrt, scale=1.0 / C,
            bias=c_eps[:]),
             reads=[sB, bC], writes=[sB])

    def emit_ln_b(i):
        p = i % 4
        xb = bXld[p]
        sB, sT = bStat[p], stat[p]
        P.op("vector", lambda v, sT=sT: v.reciprocal(sT[:, 4:5], sT[:, 3:4]),
             reads=[sB], writes=[sB])
        P.op("gpsimd", lambda g, i=i, p=p, sT=sT: g.tensor_scalar(
            xall[:, C * i:C * (i + 1)], xld[p][:], sT[:, 1:2], sT[:, 4:5],
            ALU.add, ALU.mult),
             reads=[xb, sB], writes=[bXall[i]])
        if i + 4 < NT:
            x_load(i + 4)

    def emit_front(i):
        p = i % 4
        p2 = i % 2
        TB = B6 if p2 == 0 else B7
        P.grp("tensor", [
            (lambda t, i=i, j=j, TB=TB: t.transpose(
                TB[:, 128 * j:128 * (j + 1)],
                xall[:, C * i + 128 * j:C * i + 128 * (j + 1)], c_identbf[:]))
            for j in range(NC8)],
            reads=[bXall[i], bC], writes=[pXnT[p2]])
        P.op("scalar", lambda sc, p2=p2, TB=TB: sc.copy(xnT[p2][:], TB[:]),
             reads=[pXnT[p2]], writes=[bXnT[p2]])
        Bq = B0 if p2 == 0 else B1
        xT = xnT[p2]
        fns = []
        fns += [(lambda t, j=j, Bq=Bq, xT=xT: t.matmul(
            Bq[:, 0:128], wqk[:, 128 * j:128 * (j + 1)],
            xT[:, 128 * j:128 * (j + 1)], start=(j == 0), stop=(j == NC8 - 1)))
            for j in range(NC8)]
        fns += [(lambda t, j=j, Bq=Bq, xT=xT: t.matmul(
            Bq[:, 128:192], xT[:, 128 * j:128 * (j + 1)],
            wqk[:, 128 * j + 64:128 * (j + 1)], start=(j == 0), stop=False))
            for j in range(NC8)]
        fns += [lambda t, Bq=Bq: t.matmul(
            Bq[:, 128:192], c_ones1b[0:1, 0:128], kvbrow[0:1, 0:64],
            start=False, stop=True)]
        fns += [(lambda t, j=j, Bq=Bq, xT=xT: t.matmul(
            Bq[:, 192:256], xT[:, 128 * j:128 * (j + 1)],
            wv[:, 64 * j:64 * (j + 1)], start=(j == 0), stop=False))
            for j in range(NC8)]
        fns += [lambda t, Bq=Bq: t.matmul(
            Bq[:, 192:256], c_ones1b[0:1, 0:128], kvbrow[0:1, 64:128],
            start=False, stop=True)]
        fns += [(lambda t, j=j, Bq=Bq, xT=xT: t.matmul(
            Bq[:, 256:256 + E], xT[:, 128 * j:128 * (j + 1)],
            wrt[:, E * j:E * (j + 1)], start=(j == 0), stop=(j == NC8 - 1)))
            for j in range(NC8)]
        P.grp("tensor", fns, reads=[bW, bXnT[p2], bC, bKvb], writes=[pQkv[p2]])
        P.op("scalar", lambda sc, p=p, Bq=Bq: sc.activation(
            out=qkvt[p][0:64, 0:128], in_=Bq[0:64, 0:128], func=AF.Identity,
            bias=qb[:]),
            reads=[pQkv[p2], bW], writes=[bQkvt[p]])
        P.op("scalar", lambda sc, p=p, Bq=Bq: sc.activation(
            out=qkvt[p][0:64, 128:256], in_=Bq[64:128, 0:128], func=AF.Identity,
            bias=kb[:]),
            reads=[pQkv[p2], bW], writes=[bQkvt[p]])
        P.op("scalar", lambda sc, p=p, Bq=Bq: sc.copy(
            ktv[p][:], Bq[:, 128:256]),
            reads=[pQkv[p2]], writes=[bKtv[p]])
        P.op("vector", lambda v, Bq=Bq: v.tensor_tensor(
            logit[:], Bq[:, 256:256 + E], rbrep[:], ALU.add),
            reads=[pQkv[p2], bW], writes=[bLog])
        P.op("vector", lambda v: v.max(mx8[:], logit[:]), reads=[bLog], writes=[bMx])
        P.op("vector", lambda v: v.max_index(mi8[:], mx8[:], logit[:]),
             reads=[bMx, bLog], writes=[bMi])
        P.op("vector", lambda v: v.tensor_copy(ecol[:], mi8[:, 0:1]),
             reads=[bMi], writes=[bEc])
        P.op("vector", lambda v, i=i: v.tensor_scalar(
            ohall[:, E * i:E * (i + 1)], c_iota32[:], ecol[:], None, ALU.is_equal),
            reads=[bC, bEc], writes=[bOh[i]])
    def emit_cnt(i):
        P.op("vector", lambda v, i=i: v.tensor_copy(
            runrow[0:1, E * i:E * (i + 1)], runcur[:]),
            reads=[bRunc], writes=[bRun[i]])
        P.op("tensor", lambda t, i=i: t.matmul(
            B5[0:1, 128:128 + E], c_onescolb[:], ohall[:, E * i:E * (i + 1)],
            start=True, stop=True),
            reads=[bC, bOh[i]], writes=[pCnt])
        P.op("vector", lambda v: v.tensor_tensor(
            runcur[:], runcur[:], B5[0:1, 128:128 + E], ALU.add),
            reads=[pCnt, bRunc], writes=[bRunc])

    def emit_attn(i):
        p = i % 4
        p2 = i % 2
        BA = B2 if p2 == 0 else B3
        P.op("tensor", lambda t, p=p, BA=BA: t.matmul(
            BA[:, 0:128], qkvt[p][0:64, 128:256], qkvt[p][0:64, 0:128],
            start=True, stop=True),
            reads=[bQkvt[p]], writes=[pA[p2]])
        P.op("vector", lambda v, p=p, BA=BA: v.tensor_tensor(
            A_bf[p2][:], BA[:, 0:128], c_utmask[:], ALU.mult),
            reads=[pA[p2], bC], writes=[bA[p2]])
        ofns = []
        if i > 0:
            ofns.append(lambda t, p=p: t.matmul(
                B4[0:64, 0:128], Sbf[:], qkvt[p][0:64, 0:128],
                start=True, stop=False))
        ofns.append(lambda t, p=p, p2=p2, i=i: t.matmul(
            B4[0:64, 0:128], ktv[p][:, 64:128], A_bf[p2][:],
            start=(i == 0), stop=True))
        P.grp("tensor", ofns, reads=[bSbf, bQkvt[p], bKtv[p], bA[p2]], writes=[pO])
        P.op("scalar", lambda sc, i=i: sc.copy(
            OT[64 * (i % 2):64 * (i % 2) + 64, 128 * (i // 2):128 * (i // 2) + 128],
            B4[0:64, 0:128]),
            reads=[pO], writes=[bOT[i]])
        if i < NT - 1:
            P.op("tensor", lambda t, p=p: t.matmul(
                B5[0:64, 0:64], ktv[p][:, 0:64], ktv[p][:, 64:128],
                start=True, stop=True),
                reads=[bKtv[p]], writes=[pS])
            P.op("vector", lambda v: v.tensor_tensor(
                S32[:], S32[:], B5[0:64, 0:64], ALU.add),
                reads=[pS, bS32], writes=[bS32])
            P.op("vector", lambda v: v.tensor_copy(Sbf[:], S32[:]),
                 reads=[bS32], writes=[bSbf])

    for k in range(min(4, NT)):
        emit_ln_a(k)
        if k < 2:
            emit_ln_b(k)
    zq = list(range(NZ))
    for k2 in range(0, NT, 2):
        for t in (k2 + 2, k2 + 3):
            if t < NT:
                emit_ln_b(t)
        emit_front(k2)
        if k2 + 1 < NT:
            emit_front(k2 + 1)
        if k2 > 0:
            emit_attn(k2 - 1)
        emit_attn(k2)
        emit_cnt(k2)
        if k2 + 1 < NT:
            emit_cnt(k2 + 1)
        for t in (k2 + 4, k2 + 5):
            if t < NT:
                emit_ln_a(t)
        nz = (NZ + NT - 1) // NT
        for k in zq[:nz]:
            emit_zero_chunk(k)
        del zq[:nz]
    if NT > 1:
        emit_attn(NT - 1)
    for k in zq:
        emit_zero_chunk(k)

    P.op("tensor", lambda t: t.transpose(B5[0:E, 192:193], runcur[0:1, 0:E],
                                         c_identf[0:1, 0:1]),
         reads=[bRunc, bC], writes=[pSrt])
    P.op("vector", lambda v: v.tensor_copy(cntcol[:], B5[0:E, 192:193]),
         reads=[pSrt], writes=[bSort])
    P.op("vector", lambda v: v.tensor_scalar(grid[:], c_thrP[:], cntcol[:], None,
                                             ALU.is_lt),
         reads=[bC, bSort], writes=[bSort])
    P.op("vector", lambda v: v.reduce_sum(padded[:], grid[:], axis=AX.X),
         reads=[bSort], writes=[bSort])
    P.op("vector", lambda v: v.tensor_scalar(padded[:], padded[:], float(SEG), None,
                                             ALU.mult),
         reads=[bSort], writes=[bSort])
    P.op("tensor", lambda t: t.matmul(B5[0:1, 224:224 + E], padded[:], c_u32[:],
                                      start=True, stop=True),
         reads=[bSort, bC], writes=[pSrt])
    P.op("vector", lambda v: v.tensor_copy(baser[:], B5[0:1, 224:224 + E]),
         reads=[pSrt], writes=[bSort])
    P.op("tensor", lambda t: t.matmul(B5[0:E, 193:194], c_u32[:], padded[:],
                                      start=True, stop=True),
         reads=[bSort, bC], writes=[pSrt])
    P.op("vector", lambda v: v.tensor_copy(basec[:], B5[0:E, 193:194]),
         reads=[pSrt], writes=[bSort])
    P.op("vector", lambda v: v.tensor_tensor(bpc[:], basec[:], padded[:], ALU.add),
         reads=[bSort], writes=[bSort])
    P.op("vector", lambda v: v.tensor_scalar(segoh[:], c_thrS[:], basec[:], None,
                                             ALU.is_ge),
         reads=[bC, bSort], writes=[bSort])
    P.op("vector", lambda v: v.tensor_scalar(segtmp[:], c_thrS[:], bpc[:], None,
                                             ALU.is_lt),
         reads=[bC, bSort], writes=[bSort])
    P.op("vector", lambda v: v.tensor_tensor(segoh[:], segoh[:], segtmp[:], ALU.mult),
         reads=[bSort], writes=[bSort])
    P.op("tensor", lambda t: t.matmul(B5[0:1, 224 + E:224 + E + NSEG], c_iotaE[:],
                                      segoh[:], start=True, stop=True),
         reads=[bC, bSort], writes=[pSrt])
    P.op("vector", lambda v: v.tensor_copy(seger[:], B5[0:1, 224 + E:224 + E + NSEG]),
         reads=[pSrt], writes=[bSort])
    P.op("tensor", lambda t: t.matmul(B5[0:1, 224 + E:224 + E + NSEG],
                                      c_onescol[0:E, :], segoh[:],
                                      start=True, stop=True),
         reads=[bC, bSort], writes=[pSrt])
    P.op("vector", lambda v: v.tensor_copy(segliv[:], B5[0:1, 224 + E:224 + E + NSEG]),
         reads=[pSrt], writes=[bSort])
    P.op("vector", lambda v: v.tensor_scalar(segdead[:], segliv[:], -OOB, OOB,
                                             ALU.mult, ALU.add),
         reads=[bSort], writes=[bSort])
    P.op("vector", lambda v: v.tensor_scalar(segoff[:], seger[:], 128.0, None,
                                             ALU.mult),
         reads=[bSort], writes=[bSort])
    P.op("vector", lambda v: v.tensor_tensor(segoff[:], segoff[:], segdead[:],
                                             ALU.add),
         reads=[bSort], writes=[bSort])
    P.op("tensor", lambda t: t.matmul(B2[:, 64:64 + NSEG], c_ones1f[:], segoff[:],
                                      start=True, stop=True),
         reads=[bC, bSort], writes=[pRank])
    P.op("vector", lambda v: v.tensor_scalar(idxf[:], B2[:, 64:64 + NSEG],
                                             c_iotaP[:], None, ALU.add),
         reads=[pRank, bC], writes=[bIdxw])
    P.op("vector", lambda v: v.tensor_copy(idxw[:], idxf[:]),
         reads=[bIdxw], writes=[bIdxw])
    P.op("vector", lambda v: v.tensor_tensor(segtmp[0:1, :], seger[:], segdead[:],
                                             ALU.add),
         reads=[bSort], writes=[bSort])
    P.op("tensor", lambda t: t.matmul(B2[0:2, 64:64 + NSEG], c_ones1f[0:1, 0:2],
                                      segtmp[0:1, :], start=True, stop=True),
         reads=[bC, bSort], writes=[pRank])
    P.op("vector", lambda v: v.tensor_copy(idxe[:], B2[0:2, 64:64 + NSEG]),
         reads=[pRank], writes=[bIdxw])

    bndcell = {}

    def mk_bnd(g):
        r = g.alloc_register("gbnd")
        g.reg_mov(r, E * 128 - 1)
        bndcell["v"] = g.snap(r)
        r2 = g.alloc_register("gbnd2")
        g.reg_mov(r2, E - 1)
        bndcell["e"] = g.snap(r2)
        return g.engine_nop()

    P.op("gpsimd", mk_bnd, inc=False)

    def seg_wload(s):
        p = s % 4
        P.op("gpsimd", lambda g, s=s, p=p: g.indirect_dma_start(
            out=wpks[p][:], out_offset=None,
            in_=wpk_d, in_offset=IOA(idxw[:, s:s + 1], 0),
            bounds_check=bndcell["v"], oob_is_err=False),
            reads=[bIdxw], writes=[bWpk[p]], dma=True, sem=f"d_ws{p}")
        P.op("gpsimd", lambda g, s=s, p=p: g.indirect_dma_start(
            out=b2r[p][:], out_offset=None,
            in_=b2r_d, in_offset=IOA(idxe[0:2, s:s + 1], 0),
            bounds_check=bndcell["e"], oob_is_err=False),
            reads=[bIdxw], writes=[bB2r[p]], dma=True, sem=f"d_ws{p}")
        for bb in (bWpk[p], bB2r[p]):
            P.settle(bb, f"d_ws{p}")

    def seg_xload(s):
        p = s % 4
        P.op("sync", lambda sy, s=s, p=p: sy.dma_start(
            out=xns[p][:].rearrange("p (q c) -> p q c", c=C),
            in_=xnsort_d[SEG * s:SEG * (s + 1), :].rearrange("(q p) c -> p q c", p=128)),
            writes=[bXns[p]], dma=True, sem=f"d_xs{p}",
            waits=[P.bar("d_sct"), P.bar("d_z")])

    def seg_loads(s):
        seg_wload(s)
        seg_xload(s)


    for s0 in range(min(3, NSEG)):
        seg_wload(s0)

    for i in range(NT):
        P.op("vector", lambda v, i=i: v.tensor_tensor(
            brt[:], baser[:], runrow[0:1, E * i:E * (i + 1)], ALU.add),
            reads=[bSort, bRun[i]], writes=[bBrt])
        P.grp("tensor", [
            (lambda t, i=i: t.matmul(
                B2[:, 0:E], c_ustrict[:], ohall[:, E * i:E * (i + 1)],
                start=True, stop=False)),
            (lambda t: t.matmul(B2[:, 0:E], c_ones1f[:], brt[:],
                                start=False, stop=True))],
            reads=[bC, bOh[i], bBrt], writes=[pRank])
        P.op("vector", lambda v, i=i: v.tensor_tensor(
            maskd[:], ohall[:, E * i:E * (i + 1)], B2[:, 0:E], ALU.mult),
            reads=[bOh[i], pRank], writes=[bMaskd])
        P.op("vector", lambda v: v.reduce_sum(rankf[:], maskd[:], axis=AX.X),
             reads=[bMaskd], writes=[bRankf])
        P.op("vector", lambda v, i=i: v.tensor_copy(rank32[:, i:i + 1], rankf[:]),
             reads=[bRankf], writes=[bRk[i]])
    for i in range(NT):
        P.op("gpsimd", lambda g, i=i: g.indirect_dma_start(
            out=xnsort_d, out_offset=IOA(rank32[:, i:i + 1], 0),
            in_=xall[:, C * i:C * (i + 1)], in_offset=None),
            reads=[bRk[i], bXall[i]], dma=True, sem="d_sct",
            waits=[P.bar("d_z")])

    for s0 in range(min(3, NSEG)):
        seg_xload(s0)
    def seg_tr_q(s, q):
        p = s % 4
        p2 = s % 2
        TB = B6 if p2 == 0 else B7
        if q == 0:
            P.op("vector", lambda v, p=p: v.tensor_copy(
                bcol[p][:], wpks[p][:, 4 * C:4 * C + 2]),
                reads=[bWpk[p]], writes=[bBcol[p]])
        if q >= SQ:
            return
        P.grp("tensor", [
            (lambda t, p=p, q=q, j=j, TB=TB: t.transpose(
                TB[:, 128 * j:128 * (j + 1)],
                xns[p][:, C * q + 128 * j:C * q + 128 * (j + 1)],
                c_identbf[:])) for j in range(NC8)],
            reads=[bXns[p], bC], writes=[pXnT[p2]],
            waits=[P.bar(f"d_xs{p}")] if q == 0 else ())
        P.op("vector", lambda v, p2=p2, q=q, TB=TB: v.tensor_copy(
            xnsT[p2][:, C * q:C * (q + 1)], TB[:]),
            reads=[pXnT[p2]], writes=[bXnsT[p2]])

    def seg_tr(s):
        for q in range(SQ):
            seg_tr_q(s, q)

    def seg_h(s):
        p = s % 4
        p2 = s % 2
        BH = B0 if p2 == 0 else B1
        xnsT4 = xnsT[p2][:].rearrange("p (q j t) -> p q j t", q=SQ, j=NC8)
        for hh in range(2):
            P.grp("tensor", [
                (lambda t, p=p, p2=p2, hh=hh, j=j, BH=BH, xnsT4=xnsT4: t.matmul(
                    BH[:, SEG * hh:SEG * (hh + 1)],
                    wpks[p][:, H * j + 128 * hh:H * j + 128 * (hh + 1)],
                    xnsT4[:, :, j, :], start=(j == 0), stop=(j == NC8 - 1)))
                for j in range(NC8)],
                reads=[bWpk[p], bXnsT[p2]], writes=[pH[p2]],
                waits=[P.bar(f"d_ws{p}")] if hh == 0 else ())
            P.op("scalar", lambda sc, p=p, p2=p2, hh=hh, BH=BH: sc.activation(
                out=hfm[p2][:, SEG * hh:SEG * (hh + 1)],
                in_=BH[:, SEG * hh:SEG * (hh + 1)], func=AF.Relu,
                bias=bcol[p][:, hh:hh + 1]),
                reads=[pH[p2], bBcol[p]], writes=[bHfm[p2]])

    def seg_y(s, interleave=None):
        p = s % 4
        p2 = s % 2
        for q in range(SQ):
            if interleave is not None:
                interleave(q)
            for cb in range(2):
                yb = (q * 2 + cb) % 2
                BY = B2 if yb == 0 else B3
                yfns = [(lambda t, p=p, p2=p2, q=q, cb=cb, j=j, BY=BY: t.matmul(
                    BY[:, 0:512],
                    hfm[p2][:, SEG * j + 128 * q:SEG * j + 128 * (q + 1)],
                    wpks[p][:, 2 * C + C * j + 512 * cb:2 * C + C * j + 512 * (cb + 1)],
                    start=(j == 0), stop=False)) for j in range(2)]
                yfns.append(lambda t, p=p, cb=cb, BY=BY: t.matmul(
                    BY[:, 0:512], c_ones1b[0:1, 0:128],
                    b2r[p][0:1, 512 * cb:512 * (cb + 1)], start=False, stop=True))
                P.grp("tensor", yfns,
                      reads=[bWpk[p], bHfm[p2], bB2r[p], bC], writes=[pY[yb]])
                if (q * 2 + cb) % 2 == 0:
                    P.op("scalar", lambda sc, p2=p2, q=q, cb=cb, BY=BY: sc.activation(
                        out=ytm[p2][:, C * q + 512 * cb:C * q + 512 * (cb + 1)],
                        in_=BY[:, 0:512], func=AF.Relu),
                        reads=[pY[yb]], writes=[bYtm[p2]])
                else:
                    P.op("vector", lambda v, p2=p2, q=q, cb=cb, BY=BY: v.tensor_scalar(
                        ytm[p2][:, C * q + 512 * cb:C * q + 512 * (cb + 1)],
                        BY[:, 0:512], 0.0, None, ALU.max),
                        reads=[pY[yb]], writes=[bYtm[p2]])
        P.op("sync", lambda sy, s=s, p2=p2: sy.dma_start(
            out=ysort_d[SEG * s:SEG * (s + 1), :].rearrange("(q p) c -> p q c", p=128),
            in_=ytm[p2][:].rearrange("p (q c) -> p q c", c=C)),
            reads=[bYtm[p2]], dma=True, sem=f"d_ys{p2}")

    seg_tr(0)
    for s in range(NSEG):
        seg_h(s)
        if s + 1 < NSEG:
            if s > 0:
                seg_y(s - 1, interleave=lambda k, s=s: seg_tr_q(s + 1, k))
            else:
                seg_tr(1)
        elif s > 0:
            seg_y(s - 1)
        if s + 3 < NSEG:
            seg_loads(s + 3)
    seg_y(NSEG - 1)

    p2_gate = [P.bar("d_ys0"), P.bar("d_ys1")]

    def p3_loads(i):
        p = i % 4
        py = i % 3
        P.op("sync", lambda s, i=i, p=p: s.dma_start(
            out=xld[p][:], in_=x_d[128 * i:128 * (i + 1), :]),
            writes=[bXld[p]], dma=True, sem=f"d_x{p}")
        P.op("gpsimd", lambda g, i=i, py=py: g.indirect_dma_start(
            out=yt[py][:], out_offset=None,
            in_=ysort_d, in_offset=IOA(rank32[:, i:i + 1], 0)),
            reads=[bRk[i]], writes=[bYt[py]], dma=True, sem=f"d_yg{py}",
            waits=p2_gate)

    def p3_mm_xadd(i):
        p = i % 4
        po = i % 2
        Bo = [B4, B5] if po == 0 else [B2, B3]
        pOutp = pOut[po]
        for hf in range(2):
            P.grp("tensor", [
                (lambda t, i=i, hf=hf, Bo=Bo: t.matmul(
                    Bo[hf][:, 0:512],
                    OT[64 * (i % 2):64 * (i % 2) + 64,
                       128 * (i // 2):128 * (i // 2) + 128],
                    pefft[64 * (i % 2):64 * (i % 2) + 64,
                          512 * hf:512 * (hf + 1)], start=True, stop=False)),
                (lambda t, hf=hf, Bo=Bo: t.matmul(
                    Bo[hf][:, 0:512], c_ones1b[0:1, 0:128],
                    projb_b[0:1, 512 * hf:512 * (hf + 1)], start=False, stop=True))],
                reads=[bOT[i], bW, bC], writes=[pOutp[hf]], waits=p2_gate)
        for hf in range(2):
            P.op("vector", lambda v, p=p, po=po, hf=hf, Bo=Bo: v.tensor_tensor(
                outt[po][:, 512 * hf:512 * (hf + 1)], Bo[hf][:, 0:512],
                xld[p][:, 512 * hf:512 * (hf + 1)], ALU.add),
                reads=[pOutp[hf], bXld[p]], writes=[bOutt[po]],
                waits=[P.bar(f"d_x{p}")])

    def p3_yadd_store(j):
        py = j % 3
        po = j % 2
        for hf in range(2):
            P.op("vector", lambda v, py=py, po=po, hf=hf: v.tensor_tensor(
                outt[po][:, 512 * hf:512 * (hf + 1)],
                outt[po][:, 512 * hf:512 * (hf + 1)],
                yt[py][:, 512 * hf:512 * (hf + 1)], ALU.add),
                reads=[bYt[py], bOutt[po]], writes=[bOutt[po]],
                waits=[P.bar(f"d_yg{py}")])
        P.op("sync", lambda s, j=j, po=po: s.dma_start(
            out=out_d[128 * j:128 * (j + 1), :], in_=outt[po][:]),
            reads=[bOutt[po]], dma=True, sem=f"d_out{po}")

    for i0 in range(min(2, NT)):
        p3_loads(i0)
    for i in range(NT):
        p3_mm_xadd(i)
        if i > 0:
            p3_yadd_store(i - 1)
        if i + 2 < NT:
            p3_loads(i + 2)
    p3_yadd_store(NT - 1)

    P.op("vector", lambda v: v.engine_nop(),
         waits=[P.bar("d_out0"), P.bar("d_out1")], inc=False)
    P.build(st)
    st.close()
    return nc


def pack_ffn(w1, b1, w2, b2):
    E, C, H = 32, 1024, 256
    NC8 = C // 128
    w1 = np.asarray(w1, dtype=np.float32)
    b1 = np.asarray(b1, dtype=np.float32)
    w2 = np.asarray(w2, dtype=np.float32)
    b2 = np.asarray(b2, dtype=np.float32)
    w1p = w1.reshape(E, NC8, 128, H).transpose(0, 2, 1, 3).reshape(E, 128, NC8 * H)
    w2p = w2.reshape(E, 2, 128, C).transpose(0, 2, 1, 3).reshape(E, 128, 2 * C)
    b1p = b1.reshape(E, 2, 128).transpose(0, 2, 1)
    b2p = b2.reshape(E, NC8, 128).transpose(0, 2, 1)
    wpk = np.concatenate([w1p, w2p, b1p, b2p], axis=2).reshape(E * 128, 4 * C + 2 + NC8)
    return {"wpk": np.ascontiguousarray(wpk), "b2row": np.ascontiguousarray(b2)}


def make_consts(T=4096, SEG=256):
    E = 32
    NSEG = T // SEG + E
    NPT = T // SEG
    bf16 = ml_dtypes.bfloat16
    ident = np.eye(128, dtype=np.float32)
    k = np.arange(128)
    utmask = (k[:, None] <= k[None, :]).astype(np.float32)
    ustrict = (k[:, None] < k[None, :]).astype(np.float32)
    e = np.arange(E)
    u32 = (e[:, None] < e[None, :]).astype(np.float32)
    return {
        "identbf": ident.astype(bf16),
        "identf": ident,
        "utmask": utmask.astype(bf16),
        "iota32": np.tile(np.arange(E, dtype=np.float32), (128, 1)),
        "iotaE": e.astype(np.float32).reshape(E, 1),
        "iotaP": np.arange(128, dtype=np.float32).reshape(128, 1),
        "u32": u32,
        "ustrict": ustrict.astype(bf16),
        "thrP": np.tile((np.arange(NPT) * SEG).astype(np.float32), (E, 1)),
        "thrS": np.tile((np.arange(NSEG) * SEG).astype(np.float32), (E, 1)),
    }


def kernel(L0_attn_ids, L0_attn_qkv_w, L0_attn_qkv_b, L0_attn_proj_w, L0_attn_proj_b,
           L0_ffn_router_w, L0_ffn_router_b, L0_ffn_weights1, L0_ffn_bias1,
           L0_ffn_weights2, L0_ffn_bias2):
    T, C, E, H = 4096, 1024, 32, 256
    SEG = 256
    x = np.ascontiguousarray(np.asarray(L0_attn_ids, dtype=np.float32))
    weights = {
        "qkv_w": np.ascontiguousarray(np.asarray(L0_attn_qkv_w, dtype=np.float32)),
        "qkv_b": np.asarray(L0_attn_qkv_b, dtype=np.float32).reshape(1, 192),
        "proj_w": np.ascontiguousarray(np.asarray(L0_attn_proj_w, dtype=np.float32)),
        "proj_b": np.asarray(L0_attn_proj_b, dtype=np.float32).reshape(1, C),
        "router_w": np.ascontiguousarray(np.asarray(L0_ffn_router_w, dtype=np.float32)),
        "router_b": np.asarray(L0_ffn_router_b, dtype=np.float32).reshape(1, E),
    }
    weights.update(pack_ffn(L0_ffn_weights1, L0_ffn_bias1,
                            L0_ffn_weights2, L0_ffn_bias2))
    weights.update(make_consts(T, SEG))

    nc = bass.Bass("TRN2", target_bir_lowering=False)
    build_program(nc, T=T, SEG=SEG)

    in_maps = []
    for b in range(8):
        m = {"x": x[b]}
        m.update(weights)
        in_maps.append(m)
    res = run_bass_kernel_spmd(nc, in_maps, core_ids=list(range(8)))
    return np.stack([r["out"] for r in res.results], axis=0)


# version 38 (final)
# speedup vs baseline: 1.4465x; 1.0896x over previous
import contextlib

import numpy as np
import ml_dtypes

import concourse.bass as bass
import concourse.mybir as mybir
from concourse.bass_utils import run_bass_kernel_spmd

F32 = mybir.dt.float32
BF16 = mybir.dt.bfloat16
I32 = mybir.dt.int32
U32 = mybir.dt.uint32
AF = mybir.ActivationFunctionType
ALU = mybir.AluOpType
AX = mybir.AxisListType
DS = bass.ds
IOA = bass.IndirectOffsetOnAxis

ENGINES = ("sync", "gpsimd", "tensor", "vector", "scalar")


class Buf:

    __slots__ = ("name", "w", "r")

    def __init__(self, name):
        self.name = name
        self.w = {}
        self.r = {}


class Prog:

    def __init__(self, nc):
        self.nc = nc
        self.items = {e: [] for e in ENGINES}
        self.counts = {}
        self.qwm = {e: {} for e in ENGINES}

    def bar(self, sem):
        return (sem, self.counts.get(sem, 0))

    def _mkwaits(self, eng, reads, writes, waits):
        w = {}

        def add(t):
            if t is None:
                return
            s, c = t
            if w.get(s, 0) < c:
                w[s] = c

        for t in waits:
            add(t)
        for b in reads:
            for it in b.w.items():
                add(it)
        for b in writes:
            for it in b.w.items():
                add(it)
            for it in b.r.items():
                add(it)
        wm = self.qwm[eng]
        for s in list(w):
            c = w[s]
            if s.startswith("d_"):
                if c <= wm.get(s, 0):
                    del w[s]
                    continue
                assert c == self.counts.get(s, 0), (
                    f"non-barrier wait on DMA sem {s}: {c} != {self.counts.get(s, 0)}"
                )
            if wm.get(s, 0) < c:
                wm[s] = c
        return w

    def _record(self, reads, writes, tok):
        for b in reads:
            if b.r.get(tok[0], 0) < tok[1]:
                b.r[tok[0]] = tok[1]
        for b in writes:
            if b.w.get(tok[0], 0) < tok[1]:
                b.w[tok[0]] = tok[1]
            b.r = {}

    def op(self, eng, fn, reads=(), writes=(), waits=(), dma=False, sem=None, inc=True):
        sem = sem or (("d_" if dma else "s_") + eng)
        amt = 16 if dma else 1
        w = self._mkwaits(eng, reads, writes, waits)
        tok = None
        if inc:
            self.counts[sem] = self.counts.get(sem, 0) + amt
            tok = (sem, self.counts[sem])
        if reads or writes:
            assert tok is not None
        self._record(reads, writes, tok) if tok else None
        self.items[eng].append((fn, w, sem if inc else None, amt))
        return tok

    def grp(self, eng, fns, reads=(), writes=(), waits=()):
        sem = "s_" + eng
        w = self._mkwaits(eng, reads, writes, waits)
        for k, fn in enumerate(fns):
            if k == len(fns) - 1:
                self.counts[sem] = self.counts.get(sem, 0) + 1
                tok = (sem, self.counts[sem])
                self.items[eng].append((fn, w if len(fns) == 1 else {}, sem, 1))
            else:
                self.items[eng].append((fn, w if k == 0 else {}, None, 1))
        self._record(reads, writes, tok)
        return tok

    def settle(self, buf, sem):
        buf.w = {sem: self.counts.get(sem, 0)}

    def build(self, stack):
        nc = self.nc
        used = set()
        for e in ENGINES:
            for _, w, s, _ in self.items[e]:
                if s:
                    used.add(s)
                used.update(w.keys())
        sems = {s: stack.enter_context(nc.semaphore(s)) for s in sorted(used)}
        block = stack.enter_context(nc.Block())

        def mk(eng_name):
            items = self.items[eng_name]

            def section(eng):
                seen = {}
                for fn, waits, inc_sem, amt in items:
                    for s, c in waits.items():
                        if seen.get(s, 0) < c:
                            eng.wait_ge(sems[s], c)
                            seen[s] = c
                    inst = fn(eng)
                    if inc_sem:
                        inst.then_inc(sems[inc_sem], amt)

            return section

        for e in ENGINES:
            if self.items[e]:
                getattr(block, e)(mk(e))


def build_program(nc, T=4096, SEG=256):
    C, HS, E, H = 1024, 64, 32, 256
    NT = T // 128
    NC8 = C // 128
    NSEG = T // SEG + E
    SQ = SEG // 128
    SROWS = NSEG * SEG
    NPT = T // SEG
    OOB = float(1 << 25)

    dt = nc.dram_tensor
    x_d = dt("x", [T, C], F32, kind="ExternalInput").ap()
    qkvw_d = dt("qkv_w", [192, C], F32, kind="ExternalInput").ap()
    qkvb_d = dt("qkv_b", [1, 192], F32, kind="ExternalInput").ap()
    projw_d = dt("proj_w", [C, C], F32, kind="ExternalInput").ap()
    projb_d = dt("proj_b", [1, C], F32, kind="ExternalInput").ap()
    rw_d = dt("router_w", [E, C], F32, kind="ExternalInput").ap()
    rb_d = dt("router_b", [1, E], F32, kind="ExternalInput").ap()
    WPC = 4 * C + 2 + NC8
    wpk_d = dt("wpk", [E * 128, WPC], F32, kind="ExternalInput").ap()
    b2r_d = dt("b2row", [E, C], F32, kind="ExternalInput").ap()
    identbf_d = dt("identbf", [128, 128], BF16, kind="ExternalInput").ap()
    identf_d = dt("identf", [128, 128], F32, kind="ExternalInput").ap()
    utmask_d = dt("utmask", [128, 128], BF16, kind="ExternalInput").ap()
    iota32_d = dt("iota32", [128, E], F32, kind="ExternalInput").ap()
    iotaE_d = dt("iotaE", [E, 1], F32, kind="ExternalInput").ap()
    iotaP_d = dt("iotaP", [128, 1], F32, kind="ExternalInput").ap()
    u32_d = dt("u32", [E, E], F32, kind="ExternalInput").ap()
    ustrict_d = dt("ustrict", [128, 128], BF16, kind="ExternalInput").ap()
    thrP_d = dt("thrP", [E, NPT], F32, kind="ExternalInput").ap()
    thrS_d = dt("thrS", [E, NSEG], F32, kind="ExternalInput").ap()
    out_d = dt("out", [T, C], F32, kind="ExternalOutput").ap()
    xnsort_d = dt("xnsort", [SROWS, C], BF16, kind="Internal").ap()
    ysort_d = dt("ysort", [SROWS, C], BF16, kind="Internal").ap()

    st = contextlib.ExitStack()
    sb = lambda name, shape, dty: st.enter_context(nc.sbuf_tensor(name, shape, dty))
    ps = lambda name, shape, dty: st.enter_context(nc.psum_tensor(name, shape, dty))

    c_identbf = sb("c_identbf", [128, 128], BF16)
    c_identf = sb("c_identf", [128, 128], F32)
    c_utmask = sb("c_utmask", [128, 128], BF16)
    c_iota32 = sb("c_iota32", [128, E], F32)
    c_iotaE = sb("c_iotaE", [E, 1], F32)
    c_u32 = sb("c_u32", [E, E], F32)
    c_ustrict = sb("c_ustrict", [128, 128], BF16)
    c_thrP = sb("c_thrP", [E, NPT], F32)
    c_thrS = sb("c_thrS", [E, NSEG], F32)
    c_onescol = sb("c_onescol", [128, 1], F32)
    c_onescolb = sb("c_onescolb", [128, 1], BF16)
    c_ones1f = sb("c_ones1f", [1, 128], F32)
    c_ones1b = sb("c_ones1b", [1, max(SEG, 512)], BF16)
    c_eps = sb("c_eps", [128, 1], F32)

    wqk = sb("wqk", [128, NC8 * 128], BF16)
    wv = sb("wv", [128, NC8 * 64], BF16)
    wrt = sb("wrt", [128, NC8 * E], BF16)
    pefft = sb("pefft", [128, C], BF16)
    projb_b = sb("projb_b", [1, C], BF16)
    qb = sb("qb", [64, 1], F32)
    kb = sb("kb", [64, 1], F32)
    vb = sb("vb", [64, 1], F32)
    rbrep = sb("rbrep", [128, E], F32)

    bstg = sb("bstg", [1, 192], F32)

    xall = sb("xall", [128, NT * C], BF16)
    OT = sb("OTbuf", [128, T // 2], BF16)
    ohall = sb("ohall", [128, NT * E], BF16)
    rank32 = sb("rank32", [128, NT], I32)
    runrow = sb("runrow", [1, NT * E], F32)
    runcur = sb("runcur", [1, E], F32)
    xld = [sb(f"xld{p}", [128, C], F32) for p in range(4)]
    stat = [sb(f"stat{p}", [128, 8], F32) for p in range(4)]
    xnT = [sb(f"xnT{p}", [128, C], BF16) for p in range(2)]
    qkvt = [sb(f"qkvt{p}", [64, 256], BF16) for p in range(4)]
    kvbrow = sb("kvbrow", [1, 128], BF16)
    ktv = [sb(f"ktv{p}", [128, 128], BF16) for p in range(4)]
    A_bf = [sb(f"A_bf{p}", [128, 128], BF16) for p in range(2)]
    S32 = sb("S32", [64, 64], F32)
    Sbf = sb("Sbf", [64, 64], BF16)
    logit = sb("logit", [128, E], F32)
    mx8 = sb("mx8", [128, 8], F32)
    mi8 = sb("mi8", [128, 8], U32)
    ecol = sb("ecol", [128, 1], F32)

    cntcol = sb("cntcol", [E, 1], F32)
    grid = sb("grid", [E, NPT], F32)
    padded = sb("padded", [E, 1], F32)
    baser = sb("baser", [1, E], F32)
    basec = sb("basec", [E, 1], F32)
    bpc = sb("bpc", [E, 1], F32)
    segoh = sb("segoh", [E, NSEG], F32)
    segtmp = sb("segtmp", [E, NSEG], F32)
    seger = sb("seger", [1, NSEG], F32)
    segliv = sb("segliv", [1, NSEG], F32)
    segdead = sb("segdead", [1, NSEG], F32)
    segoff = sb("segoff", [1, NSEG], F32)
    idxf = sb("idxf", [128, NSEG], F32)
    idxw = sb("idxw", [128, NSEG], I32)
    c_iotaP = sb("c_iotaP", [128, 1], F32)
    brt = sb("brt", [1, E], F32)
    maskd = sb("maskd", [128, E], F32)
    rankf = sb("rankf", [128, 1], F32)

    wpks = [sb(f"wpks{p}", [128, WPC], BF16) for p in range(4)]
    bcol = [sb(f"bcol{p}", [128, 2], F32) for p in range(4)]
    xns = [sb(f"xns{p}", [128, SQ * C], BF16) for p in range(4)]
    xnsT = [sb(f"xnsT{p}", [128, SQ * C], BF16) for p in range(2)]
    hfm = [sb(f"hfm{p}", [128, 2 * SEG], BF16) for p in range(2)]
    ytm = [sb(f"ytm{p}", [128, SQ * C], BF16) for p in range(2)]
    b2r = [sb(f"b2r{p}", [2, C], BF16) for p in range(4)]
    idxe = sb("idxe", [2, NSEG], I32)
    yt = [sb(f"yt{p}", [128, C], BF16) for p in range(3)]
    scr1 = sb("scr1", [128, C], F32)
    outt = [sb(f"outt{p}", [128, C], F32) for p in range(2)]
    wstg = [scr1, outt[1]]
    lndump = outt[0]
    wstgb = yt[0]

    B0 = ps("B0", [128, 512], F32)
    B1 = ps("B1", [128, 512], F32)
    B2 = ps("B2", [128, 512], F32)
    B3 = ps("B3", [128, 512], F32)
    B4 = ps("B4", [128, 512], F32)
    B5 = ps("B5", [128, 512], F32)
    B6 = ps("B6", [128, 1024], BF16)
    B7 = ps("B7", [128, 1024], BF16)

    P = Prog(nc)
    nb = lambda n: Buf(n)

    bC = nb("consts")
    bW = nb("wpro")
    bStg = [nb("wstg0"), nb("wstg1")]
    bLnd = nb("lndump")
    bStgb = nb("wstgb")
    bBstg = nb("bstg")
    bXall = [nb(f"xall{i}") for i in range(NT)]
    bOT = [nb(f"OT{i}") for i in range(NT)]
    bOh = [nb(f"oh{i}") for i in range(NT)]
    bRk = [nb(f"rk{i}") for i in range(NT)]
    bRun = [nb(f"run{i}") for i in range(NT)]
    bRunc = nb("runcur")
    bXld = [nb(f"xld{p}") for p in range(4)]
    bStat = [nb(f"stat{p}") for p in range(4)]
    bXnT = [nb(f"xnT{p}") for p in range(2)]
    bKvb = nb("kvbrow")
    bQkvt = [nb(f"qkvt{p}") for p in range(4)]
    bKtv = [nb(f"ktv{p}") for p in range(4)]
    bA = [nb(f"A_bf{p}") for p in range(2)]
    bS32 = nb("S32")
    bSbf = nb("Sbf")
    bLog = nb("logit")
    bMx = nb("mx8")
    bMi = nb("mi8")
    bEc = nb("ecol")
    bSort = nb("sorttabs")
    bIdxw = nb("idxw")
    bBrt = nb("brt")
    bMaskd = nb("maskd")
    bRankf = nb("rankf")
    bWpk = [nb(f"wpks{p}") for p in range(4)]
    bBcol = [nb(f"bcol{p}") for p in range(4)]
    bXns = [nb(f"xns{p}") for p in range(4)]
    bXnsT = [nb(f"xnsT{p}") for p in range(2)]
    bHfm = [nb(f"hfm{p}") for p in range(2)]
    bYtm = [nb(f"ytm{p}") for p in range(2)]
    bB2r = [nb(f"b2r{p}") for p in range(4)]
    bYt = [nb(f"yt{p}") for p in range(3)]
    bOutt = [nb(f"outt{p}") for p in range(2)]
    pQkv = [nb(f"pQkv{p}") for p in range(2)]
    pA = [nb(f"pA{p}") for p in range(2)]
    pO = nb("pO")
    pS = nb("pS")
    pCnt = nb("pCnt")
    pSrt = nb("pSrt")
    pRank = nb("pRank")
    pIdx = nb("pIdx")
    pPeff = nb("pPeff")
    pXnT = [nb(f"pXnT{p}") for p in range(2)]
    pH = [nb(f"pH{p}") for p in range(2)]
    pY = [nb(f"pY{p}") for p in range(2)]
    pOut = [[nb(f"pOut{par}{h}") for h in range(2)] for par in range(2)]

    for tgt, src in [
        (c_identbf, identbf_d), (c_identf, identf_d), (c_utmask, utmask_d),
        (c_iota32, iota32_d), (c_iotaE, iotaE_d), (c_u32, u32_d), (c_iotaP, iotaP_d),
        (c_ustrict, ustrict_d), (c_thrP, thrP_d), (c_thrS, thrS_d),
    ]:
        P.op("sync", lambda s, a=tgt, b=src: s.dma_start(out=a[:], in_=b),
             writes=[bC], dma=True, sem="d_c")
    P.op("vector", lambda v: v.memset(c_onescol[:], 1.0), writes=[bC])
    P.op("vector", lambda v: v.memset(c_onescolb[:], 1.0), writes=[bC])
    P.op("vector", lambda v: v.memset(c_ones1f[:], 1.0), writes=[bC])
    P.op("vector", lambda v: v.memset(c_ones1b[:], 1.0), writes=[bC])
    P.op("vector", lambda v: v.memset(c_eps[:], 1e-5), writes=[bC])
    P.op("vector", lambda v: v.memset(runcur[:], 0.0), writes=[bRunc])
    P.op("vector", lambda v: v.memset(S32[:], 0.0), writes=[bS32])
    P.op("vector", lambda v: v.memset(Sbf[:], 0.0), writes=[bSbf])

    def x_load(i):
        p = i % 4
        P.op("sync", lambda s, i=i, p=p: s.dma_start(
            out=xld[p][:], in_=x_d[128 * i:128 * (i + 1), :]),
            writes=[bXld[p]], dma=True, sem=f"d_x{p}")
    for k in range(min(4, NT)):
        x_load(k)

    P.op("gpsimd", lambda g: g.dma_start(out=wstgb[:], in_=qkvw_d[0:128, :]),
         writes=[bStgb], dma=True, sem="d_wg")
    P.grp("tensor", [
        (lambda t, j=j: t.transpose(
            B6[:, 128 * j:128 * (j + 1)], wstgb[:, 128 * j:128 * (j + 1)],
            c_identbf[:])) for j in range(NC8)],
        reads=[bStgb, bC], writes=[pXnT[0]])
    P.op("vector", lambda v: v.tensor_copy(wqk[:], B6[:]), reads=[pXnT[0]], writes=[bW])
    P.op("gpsimd", lambda g: g.dma_start(out=wstgb[0:64, :], in_=qkvw_d[128:192, :]),
         writes=[bStgb], dma=True, sem="d_wg")
    P.grp("tensor", [
        (lambda t, j=j: t.transpose(
            B6[:, 64 * j:64 * (j + 1)], wstgb[0:64, 128 * j:128 * (j + 1)],
            c_identbf[0:64, 0:64])) for j in range(NC8)],
        reads=[bStgb, bC], writes=[pXnT[0]])
    P.op("vector", lambda v: v.tensor_copy(wv[:], B6[:, 0:NC8 * 64]),
         reads=[pXnT[0]], writes=[bW])
    P.op("gpsimd", lambda g: g.dma_start(out=wstgb[0:E, :], in_=rw_d),
         writes=[bStgb], dma=True, sem="d_wg")
    P.grp("tensor", [
        (lambda t, j=j: t.transpose(
            B6[:, E * j:E * (j + 1)], wstgb[0:E, 128 * j:128 * (j + 1)],
            c_identbf[0:E, 0:E])) for j in range(NC8)],
        reads=[bStgb, bC], writes=[pXnT[0]])
    P.op("vector", lambda v: v.tensor_copy(wrt[:], B6[:, 0:NC8 * E]),
         reads=[pXnT[0]], writes=[bW])
    for cj in range(NC8):
        wp = cj % 2
        P.op("sync", lambda s, cj=cj, wp=wp: s.dma_start(
            out=wstg[wp][:], in_=projw_d[128 * cj:128 * (cj + 1), :]),
            writes=[bStg[wp]], dma=True, sem=f"d_pw{wp}")
        P.grp("tensor", [
            (lambda t, wp=wp, k=k: t.matmul(
                B4[0:64, 0:128], wstg[wp][:, 64 * k:64 * (k + 1)], c_identf[:],
                is_transpose=True, start=(k == 0), stop=(k == 15)))
            for k in range(16)],
            reads=[bStg[wp], bC], writes=[pPeff], waits=[P.bar(f"d_pw{wp}")])
        P.op("vector", lambda v, cj=cj: v.tensor_copy(
            pefft[0:64, 128 * cj:128 * (cj + 1)], B4[0:64, 0:128]),
            reads=[pPeff], writes=[bW])
        P.op("scalar", lambda sc, cj=cj: sc.copy(
            pefft[64:128, 128 * cj:128 * (cj + 1)], B4[0:64, 0:128]),
            reads=[pPeff], writes=[bW])
    P.op("sync", lambda s: s.dma_start(out=bstg[:], in_=qkvb_d),
         writes=[bBstg], dma=True, sem="d_bs")
    for tgt, lo in [(qb, 0), (kb, 64), (vb, 128)]:
        P.op("tensor", lambda t, tgt=tgt, lo=lo: t.transpose(
            B5[0:64, 256:257], bstg[0:1, lo:lo + 64], c_identf[0:1, 0:1]),
            reads=[bBstg, bC], writes=[pSrt], waits=[P.bar("d_bs")])
        P.op("vector", lambda v, tgt=tgt: v.tensor_copy(tgt[:], B5[0:64, 256:257]),
            reads=[pSrt], writes=[bW])
    P.op("vector", lambda v: v.tensor_copy(kvbrow[:], bstg[0:1, 64:192]),
         reads=[bBstg], writes=[bKvb], waits=[P.bar("d_bs")])
    P.op("sync", lambda s: s.dma_start(out=bstg[0:1, 0:E], in_=rb_d),
         writes=[bBstg], dma=True, sem="d_bs")
    P.op("tensor", lambda t: t.matmul(B5[:, 288:288 + E], c_ones1f[:], bstg[0:1, 0:E],
                                      start=True, stop=True),
         reads=[bBstg, bC], writes=[pSrt], waits=[P.bar("d_bs")])
    P.op("vector", lambda v: v.tensor_copy(rbrep[:], B5[:, 288:288 + E]),
         reads=[pSrt], writes=[bW])
    P.op("gpsimd", lambda g: g.dma_start(out=projb_b[:], in_=projb_d),
         writes=[bW], dma=True, sem="d_wg")

    P.op("vector", lambda v: v.memset(xns[0][:], 0.0), writes=[bXns[0]])

    def emit_zero_chunk(k):
        P.op("sync", lambda s, k=k: s.dma_start(
            out=xnsort_d[SQ * 128 * k:SQ * 128 * (k + 1), :]
                .rearrange("(p n) c -> p n c", p=128),
            in_=xns[0][:].rearrange("p (n c) -> p n c", c=C)),
            reads=[bXns[0]], dma=True, sem="d_z")

    NZ = SROWS // (128 * SQ)


    def emit_ln_a(i):
        p = i % 4
        xb = bXld[p]
        sB, sT = bStat[p], stat[p]
        P.op("vector", lambda v, p=p, sT=sT: v.reduce_sum(
            sT[:, 0:1], xld[p][:], axis=AX.X, negate=True),
             reads=[xb], writes=[sB], waits=[P.bar(f"d_x{p}")])
        P.op("vector", lambda v, sT=sT: v.tensor_scalar(sT[:, 1:2], sT[:, 0:1],
                                                        1.0 / C, None, ALU.mult),
             reads=[sB], writes=[sB])
        P.op("vector", lambda v, p=p, sT=sT: v.scalar_tensor_tensor(
            out=lndump[:], in0=xld[p][:], scalar=sT[:, 1:2], in1=xld[p][:],
            op0=ALU.add, op1=ALU.mult, accum_out=sT[:, 2:3]),
             reads=[xb, sB], writes=[bLnd, sB])
        P.op("scalar", lambda s, sT=sT: s.activation(
            out=sT[:, 3:4], in_=sT[:, 2:3], func=AF.Sqrt, scale=1.0 / C,
            bias=c_eps[:]),
             reads=[sB, bC], writes=[sB])

    def emit_ln_b(i):
        p = i % 4
        xb = bXld[p]
        sB, sT = bStat[p], stat[p]
        P.op("vector", lambda v, sT=sT: v.reciprocal(sT[:, 4:5], sT[:, 3:4]),
             reads=[sB], writes=[sB])
        P.op("gpsimd", lambda g, i=i, p=p, sT=sT: g.tensor_scalar(
            xall[:, C * i:C * (i + 1)], xld[p][:], sT[:, 1:2], sT[:, 4:5],
            ALU.add, ALU.mult),
             reads=[xb, sB], writes=[bXall[i]])
        if i + 4 < NT:
            x_load(i + 4)

    def emit_front(i):
        p = i % 4
        p2 = i % 2
        TB = B6 if p2 == 0 else B7
        P.grp("tensor", [
            (lambda t, i=i, j=j, TB=TB: t.transpose(
                TB[:, 128 * j:128 * (j + 1)],
                xall[:, C * i + 128 * j:C * i + 128 * (j + 1)], c_identbf[:]))
            for j in range(NC8)],
            reads=[bXall[i], bC], writes=[pXnT[p2]])
        P.op("scalar", lambda sc, p2=p2, TB=TB: sc.copy(xnT[p2][:], TB[:]),
             reads=[pXnT[p2]], writes=[bXnT[p2]])
        Bq = B0 if p2 == 0 else B1
        xT = xnT[p2]
        fns = []
        fns += [(lambda t, j=j, Bq=Bq, xT=xT: t.matmul(
            Bq[:, 0:128], wqk[:, 128 * j:128 * (j + 1)],
            xT[:, 128 * j:128 * (j + 1)], start=(j == 0), stop=(j == NC8 - 1)))
            for j in range(NC8)]
        fns += [(lambda t, j=j, Bq=Bq, xT=xT: t.matmul(
            Bq[:, 128:192], xT[:, 128 * j:128 * (j + 1)],
            wqk[:, 128 * j + 64:128 * (j + 1)], start=(j == 0), stop=False))
            for j in range(NC8)]
        fns += [lambda t, Bq=Bq: t.matmul(
            Bq[:, 128:192], c_ones1b[0:1, 0:128], kvbrow[0:1, 0:64],
            start=False, stop=True)]
        fns += [(lambda t, j=j, Bq=Bq, xT=xT: t.matmul(
            Bq[:, 192:256], xT[:, 128 * j:128 * (j + 1)],
            wv[:, 64 * j:64 * (j + 1)], start=(j == 0), stop=False))
            for j in range(NC8)]
        fns += [lambda t, Bq=Bq: t.matmul(
            Bq[:, 192:256], c_ones1b[0:1, 0:128], kvbrow[0:1, 64:128],
            start=False, stop=True)]
        fns += [(lambda t, j=j, Bq=Bq, xT=xT: t.matmul(
            Bq[:, 256:256 + E], xT[:, 128 * j:128 * (j + 1)],
            wrt[:, E * j:E * (j + 1)], start=(j == 0), stop=(j == NC8 - 1)))
            for j in range(NC8)]
        P.grp("tensor", fns, reads=[bW, bXnT[p2], bC, bKvb], writes=[pQkv[p2]])
        P.op("scalar", lambda sc, p=p, Bq=Bq: sc.activation(
            out=qkvt[p][0:64, 0:128], in_=Bq[0:64, 0:128], func=AF.Identity,
            bias=qb[:]),
            reads=[pQkv[p2], bW], writes=[bQkvt[p]])
        P.op("scalar", lambda sc, p=p, Bq=Bq: sc.activation(
            out=qkvt[p][0:64, 128:256], in_=Bq[64:128, 0:128], func=AF.Identity,
            bias=kb[:]),
            reads=[pQkv[p2], bW], writes=[bQkvt[p]])
        P.op("scalar", lambda sc, p=p, Bq=Bq: sc.copy(
            ktv[p][:], Bq[:, 128:256]),
            reads=[pQkv[p2]], writes=[bKtv[p]])
        P.op("vector", lambda v, Bq=Bq: v.tensor_tensor(
            logit[:], Bq[:, 256:256 + E], rbrep[:], ALU.add),
            reads=[pQkv[p2], bW], writes=[bLog])
        P.op("vector", lambda v: v.max(mx8[:], logit[:]), reads=[bLog], writes=[bMx])
        P.op("vector", lambda v: v.max_index(mi8[:], mx8[:], logit[:]),
             reads=[bMx, bLog], writes=[bMi])
        P.op("vector", lambda v: v.tensor_copy(ecol[:], mi8[:, 0:1]),
             reads=[bMi], writes=[bEc])
        P.op("vector", lambda v, i=i: v.tensor_scalar(
            ohall[:, E * i:E * (i + 1)], c_iota32[:], ecol[:], None, ALU.is_equal),
            reads=[bC, bEc], writes=[bOh[i]])
    def emit_cnt(i):
        P.op("vector", lambda v, i=i: v.tensor_copy(
            runrow[0:1, E * i:E * (i + 1)], runcur[:]),
            reads=[bRunc], writes=[bRun[i]])
        P.op("tensor", lambda t, i=i: t.matmul(
            B5[0:1, 128:128 + E], c_onescolb[:], ohall[:, E * i:E * (i + 1)],
            start=True, stop=True),
            reads=[bC, bOh[i]], writes=[pCnt])
        P.op("vector", lambda v: v.tensor_tensor(
            runcur[:], runcur[:], B5[0:1, 128:128 + E], ALU.add),
            reads=[pCnt, bRunc], writes=[bRunc])

    def emit_attn(i):
        p = i % 4
        p2 = i % 2
        BA = B2 if p2 == 0 else B3
        P.op("tensor", lambda t, p=p, BA=BA: t.matmul(
            BA[:, 0:128], qkvt[p][0:64, 128:256], qkvt[p][0:64, 0:128],
            start=True, stop=True),
            reads=[bQkvt[p]], writes=[pA[p2]])
        P.op("vector", lambda v, p=p, BA=BA: v.tensor_tensor(
            A_bf[p2][:], BA[:, 0:128], c_utmask[:], ALU.mult),
            reads=[pA[p2], bC], writes=[bA[p2]])
        ofns = []
        if i > 0:
            ofns.append(lambda t, p=p: t.matmul(
                B4[0:64, 0:128], Sbf[:], qkvt[p][0:64, 0:128],
                start=True, stop=False))
        ofns.append(lambda t, p=p, p2=p2, i=i: t.matmul(
            B4[0:64, 0:128], ktv[p][:, 64:128], A_bf[p2][:],
            start=(i == 0), stop=True))
        P.grp("tensor", ofns, reads=[bSbf, bQkvt[p], bKtv[p], bA[p2]], writes=[pO])
        P.op("scalar", lambda sc, i=i: sc.copy(
            OT[64 * (i % 2):64 * (i % 2) + 64, 128 * (i // 2):128 * (i // 2) + 128],
            B4[0:64, 0:128]),
            reads=[pO], writes=[bOT[i]])
        if i < NT - 1:
            P.op("tensor", lambda t, p=p: t.matmul(
                B5[0:64, 0:64], ktv[p][:, 0:64], ktv[p][:, 64:128],
                start=True, stop=True),
                reads=[bKtv[p]], writes=[pS])
            P.op("vector", lambda v: v.tensor_tensor(
                S32[:], S32[:], B5[0:64, 0:64], ALU.add),
                reads=[pS, bS32], writes=[bS32])
            P.op("vector", lambda v: v.tensor_copy(Sbf[:], S32[:]),
                 reads=[bS32], writes=[bSbf])

    for k in range(min(4, NT)):
        emit_ln_a(k)
        if k < 2:
            emit_ln_b(k)
    zq = list(range(NZ))
    for k2 in range(0, NT, 2):
        for t in (k2 + 2, k2 + 3):
            if t < NT:
                emit_ln_b(t)
        emit_front(k2)
        if k2 + 1 < NT:
            emit_front(k2 + 1)
        if k2 > 0:
            emit_attn(k2 - 1)
        emit_attn(k2)
        emit_cnt(k2)
        if k2 + 1 < NT:
            emit_cnt(k2 + 1)
        for t in (k2 + 4, k2 + 5):
            if t < NT:
                emit_ln_a(t)
        nz = (NZ + NT - 1) // NT
        for k in zq[:nz]:
            emit_zero_chunk(k)
        del zq[:nz]
    if NT > 1:
        emit_attn(NT - 1)
    for k in zq:
        emit_zero_chunk(k)

    P.op("tensor", lambda t: t.transpose(B5[0:E, 192:193], runcur[0:1, 0:E],
                                         c_identf[0:1, 0:1]),
         reads=[bRunc, bC], writes=[pSrt])
    P.op("vector", lambda v: v.tensor_copy(cntcol[:], B5[0:E, 192:193]),
         reads=[pSrt], writes=[bSort])
    P.op("vector", lambda v: v.tensor_scalar(grid[:], c_thrP[:], cntcol[:], None,
                                             ALU.is_lt),
         reads=[bC, bSort], writes=[bSort])
    P.op("vector", lambda v: v.reduce_sum(padded[:], grid[:], axis=AX.X),
         reads=[bSort], writes=[bSort])
    P.op("vector", lambda v: v.tensor_scalar(padded[:], padded[:], float(SEG), None,
                                             ALU.mult),
         reads=[bSort], writes=[bSort])
    P.op("tensor", lambda t: t.matmul(B5[0:1, 224:224 + E], padded[:], c_u32[:],
                                      start=True, stop=True),
         reads=[bSort, bC], writes=[pSrt])
    P.op("vector", lambda v: v.tensor_copy(baser[:], B5[0:1, 224:224 + E]),
         reads=[pSrt], writes=[bSort])
    P.op("tensor", lambda t: t.matmul(B5[0:E, 193:194], c_u32[:], padded[:],
                                      start=True, stop=True),
         reads=[bSort, bC], writes=[pSrt])
    P.op("vector", lambda v: v.tensor_copy(basec[:], B5[0:E, 193:194]),
         reads=[pSrt], writes=[bSort])
    P.op("vector", lambda v: v.tensor_tensor(bpc[:], basec[:], padded[:], ALU.add),
         reads=[bSort], writes=[bSort])
    P.op("vector", lambda v: v.tensor_scalar(segoh[:], c_thrS[:], basec[:], None,
                                             ALU.is_ge),
         reads=[bC, bSort], writes=[bSort])
    P.op("vector", lambda v: v.tensor_scalar(segtmp[:], c_thrS[:], bpc[:], None,
                                             ALU.is_lt),
         reads=[bC, bSort], writes=[bSort])
    P.op("vector", lambda v: v.tensor_tensor(segoh[:], segoh[:], segtmp[:], ALU.mult),
         reads=[bSort], writes=[bSort])
    P.op("tensor", lambda t: t.matmul(B5[0:1, 224 + E:224 + E + NSEG], c_iotaE[:],
                                      segoh[:], start=True, stop=True),
         reads=[bC, bSort], writes=[pSrt])
    P.op("vector", lambda v: v.tensor_copy(seger[:], B5[0:1, 224 + E:224 + E + NSEG]),
         reads=[pSrt], writes=[bSort])
    P.op("tensor", lambda t: t.matmul(B5[0:1, 224 + E:224 + E + NSEG],
                                      c_onescol[0:E, :], segoh[:],
                                      start=True, stop=True),
         reads=[bC, bSort], writes=[pSrt])
    P.op("vector", lambda v: v.tensor_copy(segliv[:], B5[0:1, 224 + E:224 + E + NSEG]),
         reads=[pSrt], writes=[bSort])
    P.op("vector", lambda v: v.tensor_scalar(segdead[:], segliv[:], -OOB, OOB,
                                             ALU.mult, ALU.add),
         reads=[bSort], writes=[bSort])
    P.op("vector", lambda v: v.tensor_scalar(segoff[:], seger[:], 128.0, None,
                                             ALU.mult),
         reads=[bSort], writes=[bSort])
    P.op("vector", lambda v: v.tensor_tensor(segoff[:], segoff[:], segdead[:],
                                             ALU.add),
         reads=[bSort], writes=[bSort])
    P.op("tensor", lambda t: t.matmul(B2[:, 64:64 + NSEG], c_ones1f[:], segoff[:],
                                      start=True, stop=True),
         reads=[bC, bSort], writes=[pRank])
    P.op("vector", lambda v: v.tensor_scalar(idxf[:], B2[:, 64:64 + NSEG],
                                             c_iotaP[:], None, ALU.add),
         reads=[pRank, bC], writes=[bIdxw])
    P.op("vector", lambda v: v.tensor_copy(idxw[:], idxf[:]),
         reads=[bIdxw], writes=[bIdxw])
    P.op("vector", lambda v: v.tensor_tensor(segtmp[0:1, :], seger[:], segdead[:],
                                             ALU.add),
         reads=[bSort], writes=[bSort])
    P.op("tensor", lambda t: t.matmul(B2[0:2, 64:64 + NSEG], c_ones1f[0:1, 0:2],
                                      segtmp[0:1, :], start=True, stop=True),
         reads=[bC, bSort], writes=[pRank])
    P.op("vector", lambda v: v.tensor_copy(idxe[:], B2[0:2, 64:64 + NSEG]),
         reads=[pRank], writes=[bIdxw])

    bndcell = {}

    def mk_bnd(g):
        r = g.alloc_register("gbnd")
        g.reg_mov(r, E * 128 - 1)
        bndcell["v"] = g.snap(r)
        r2 = g.alloc_register("gbnd2")
        g.reg_mov(r2, E - 1)
        bndcell["e"] = g.snap(r2)
        return g.engine_nop()

    P.op("gpsimd", mk_bnd, inc=False)

    def seg_wload(s):
        p = s % 4
        P.op("gpsimd", lambda g, s=s, p=p: g.indirect_dma_start(
            out=wpks[p][:], out_offset=None,
            in_=wpk_d, in_offset=IOA(idxw[:, s:s + 1], 0),
            bounds_check=bndcell["v"], oob_is_err=False),
            reads=[bIdxw], writes=[bWpk[p]], dma=True, sem=f"d_ws{p}")
        P.op("gpsimd", lambda g, s=s, p=p: g.indirect_dma_start(
            out=b2r[p][:], out_offset=None,
            in_=b2r_d, in_offset=IOA(idxe[0:2, s:s + 1], 0),
            bounds_check=bndcell["e"], oob_is_err=False),
            reads=[bIdxw], writes=[bB2r[p]], dma=True, sem=f"d_ws{p}")
        for bb in (bWpk[p], bB2r[p]):
            P.settle(bb, f"d_ws{p}")

    def seg_xload(s):
        p = s % 4
        P.op("sync", lambda sy, s=s, p=p: sy.dma_start(
            out=xns[p][:].rearrange("p (q c) -> p q c", c=C),
            in_=xnsort_d[SEG * s:SEG * (s + 1), :].rearrange("(q p) c -> p q c", p=128)),
            writes=[bXns[p]], dma=True, sem=f"d_xs{p}",
            waits=[P.bar("d_sct"), P.bar("d_z")])

    def seg_loads(s):
        seg_wload(s)
        seg_xload(s)


    for s0 in range(min(3, NSEG)):
        seg_wload(s0)

    for i in range(NT):
        P.op("vector", lambda v, i=i: v.tensor_tensor(
            brt[:], baser[:], runrow[0:1, E * i:E * (i + 1)], ALU.add),
            reads=[bSort, bRun[i]], writes=[bBrt])
        P.grp("tensor", [
            (lambda t, i=i: t.matmul(
                B2[:, 0:E], c_ustrict[:], ohall[:, E * i:E * (i + 1)],
                start=True, stop=False)),
            (lambda t: t.matmul(B2[:, 0:E], c_ones1f[:], brt[:],
                                start=False, stop=True))],
            reads=[bC, bOh[i], bBrt], writes=[pRank])
        P.op("vector", lambda v, i=i: v.tensor_tensor(
            maskd[:], ohall[:, E * i:E * (i + 1)], B2[:, 0:E], ALU.mult),
            reads=[bOh[i], pRank], writes=[bMaskd])
        P.op("vector", lambda v: v.reduce_sum(rankf[:], maskd[:], axis=AX.X),
             reads=[bMaskd], writes=[bRankf])
        P.op("vector", lambda v, i=i: v.tensor_copy(rank32[:, i:i + 1], rankf[:]),
             reads=[bRankf], writes=[bRk[i]])
    for i in range(NT):
        P.op("gpsimd", lambda g, i=i: g.indirect_dma_start(
            out=xnsort_d, out_offset=IOA(rank32[:, i:i + 1], 0),
            in_=xall[:, C * i:C * (i + 1)], in_offset=None),
            reads=[bRk[i], bXall[i]], dma=True, sem="d_sct",
            waits=[P.bar("d_z")])

    for s0 in range(min(3, NSEG)):
        seg_xload(s0)
    def seg_tr_q(s, q):
        p = s % 4
        p2 = s % 2
        TB = B6 if p2 == 0 else B7
        if q == 0:
            P.op("vector", lambda v, p=p: v.tensor_copy(
                bcol[p][:], wpks[p][:, 4 * C:4 * C + 2]),
                reads=[bWpk[p]], writes=[bBcol[p]])
        if q >= SQ:
            return
        P.grp("tensor", [
            (lambda t, p=p, q=q, j=j, TB=TB: t.transpose(
                TB[:, 128 * j:128 * (j + 1)],
                xns[p][:, C * q + 128 * j:C * q + 128 * (j + 1)],
                c_identbf[:])) for j in range(NC8)],
            reads=[bXns[p], bC], writes=[pXnT[p2]],
            waits=[P.bar(f"d_xs{p}")] if q == 0 else ())
        P.op("vector", lambda v, p2=p2, q=q, TB=TB: v.tensor_copy(
            xnsT[p2][:, C * q:C * (q + 1)], TB[:]),
            reads=[pXnT[p2]], writes=[bXnsT[p2]])

    def seg_tr(s):
        for q in range(SQ):
            seg_tr_q(s, q)

    def seg_h(s):
        p = s % 4
        p2 = s % 2
        BH = B0 if p2 == 0 else B1
        xnsT4 = xnsT[p2][:].rearrange("p (q j t) -> p q j t", q=SQ, j=NC8)
        for hh in range(2):
            P.grp("tensor", [
                (lambda t, p=p, p2=p2, hh=hh, j=j, BH=BH, xnsT4=xnsT4: t.matmul(
                    BH[:, SEG * hh:SEG * (hh + 1)],
                    wpks[p][:, H * j + 128 * hh:H * j + 128 * (hh + 1)],
                    xnsT4[:, :, j, :], start=(j == 0), stop=(j == NC8 - 1)))
                for j in range(NC8)],
                reads=[bWpk[p], bXnsT[p2]], writes=[pH[p2]],
                waits=[P.bar(f"d_ws{p}")] if hh == 0 else ())
            P.op("scalar", lambda sc, p=p, p2=p2, hh=hh, BH=BH: sc.activation(
                out=hfm[p2][:, SEG * hh:SEG * (hh + 1)],
                in_=BH[:, SEG * hh:SEG * (hh + 1)], func=AF.Relu,
                bias=bcol[p][:, hh:hh + 1]),
                reads=[pH[p2], bBcol[p]], writes=[bHfm[p2]])

    def seg_y(s, interleave=None):
        p = s % 4
        p2 = s % 2
        for q in range(SQ):
            if interleave is not None:
                interleave(q)
            for cb in range(2):
                yb = (q * 2 + cb) % 2
                BY = B2 if yb == 0 else B3
                yfns = [(lambda t, p=p, p2=p2, q=q, cb=cb, j=j, BY=BY: t.matmul(
                    BY[:, 0:512],
                    hfm[p2][:, SEG * j + 128 * q:SEG * j + 128 * (q + 1)],
                    wpks[p][:, 2 * C + C * j + 512 * cb:2 * C + C * j + 512 * (cb + 1)],
                    start=(j == 0), stop=False)) for j in range(2)]
                yfns.append(lambda t, p=p, cb=cb, BY=BY: t.matmul(
                    BY[:, 0:512], c_ones1b[0:1, 0:128],
                    b2r[p][0:1, 512 * cb:512 * (cb + 1)], start=False, stop=True))
                P.grp("tensor", yfns,
                      reads=[bWpk[p], bHfm[p2], bB2r[p], bC], writes=[pY[yb]])
                if (q * 2 + cb) % 2 == 0:
                    P.op("scalar", lambda sc, p2=p2, q=q, cb=cb, BY=BY: sc.activation(
                        out=ytm[p2][:, C * q + 512 * cb:C * q + 512 * (cb + 1)],
                        in_=BY[:, 0:512], func=AF.Relu),
                        reads=[pY[yb]], writes=[bYtm[p2]])
                else:
                    P.op("vector", lambda v, p2=p2, q=q, cb=cb, BY=BY: v.tensor_scalar(
                        ytm[p2][:, C * q + 512 * cb:C * q + 512 * (cb + 1)],
                        BY[:, 0:512], 0.0, None, ALU.max),
                        reads=[pY[yb]], writes=[bYtm[p2]])
        P.op("sync", lambda sy, s=s, p2=p2: sy.dma_start(
            out=ysort_d[SEG * s:SEG * (s + 1), :].rearrange("(q p) c -> p q c", p=128),
            in_=ytm[p2][:].rearrange("p (q c) -> p q c", c=C)),
            reads=[bYtm[p2]], dma=True, sem=f"d_ys{p2}")

    seg_tr(0)
    for s in range(NSEG):
        seg_h(s)
        if s + 1 < NSEG:
            if s > 0:
                seg_y(s - 1, interleave=lambda k, s=s: seg_tr_q(s + 1, k))
            else:
                seg_tr(1)
        elif s > 0:
            seg_y(s - 1)
        if s + 3 < NSEG:
            seg_loads(s + 3)
    seg_y(NSEG - 1)

    p2_gate = [P.bar("d_ys0"), P.bar("d_ys1")]

    def p3_loads(i):
        p = i % 4
        py = i % 3
        P.op("sync", lambda s, i=i, p=p: s.dma_start(
            out=xld[p][:], in_=x_d[128 * i:128 * (i + 1), :]),
            writes=[bXld[p]], dma=True, sem=f"d_x{p}")
        P.op("gpsimd", lambda g, i=i, py=py: g.indirect_dma_start(
            out=yt[py][:], out_offset=None,
            in_=ysort_d, in_offset=IOA(rank32[:, i:i + 1], 0)),
            reads=[bRk[i]], writes=[bYt[py]], dma=True, sem=f"d_yg{py}",
            waits=p2_gate)

    for i0 in range(min(3, NT)):
        p3_loads(i0)
    for i in range(NT):
        p = i % 4
        py = i % 3
        po = i % 2
        Bo = [B4, B5] if po == 0 else [B2, B3]
        pOutp = pOut[po]
        for hf in range(2):
            P.grp("tensor", [
                (lambda t, i=i, hf=hf, Bo=Bo: t.matmul(
                    Bo[hf][:, 0:512],
                    OT[64 * (i % 2):64 * (i % 2) + 64,
                       128 * (i // 2):128 * (i // 2) + 128],
                    pefft[64 * (i % 2):64 * (i % 2) + 64,
                          512 * hf:512 * (hf + 1)], start=True, stop=False)),
                (lambda t, hf=hf, Bo=Bo: t.matmul(
                    Bo[hf][:, 0:512], c_ones1b[0:1, 0:128],
                    projb_b[0:1, 512 * hf:512 * (hf + 1)], start=False, stop=True))],
                reads=[bOT[i], bW, bC], writes=[pOutp[hf]], waits=p2_gate)
        for hf in range(2):
            P.op("vector", lambda v, p=p, po=po, hf=hf, Bo=Bo: v.tensor_tensor(
                outt[po][:, 512 * hf:512 * (hf + 1)], Bo[hf][:, 0:512],
                xld[p][:, 512 * hf:512 * (hf + 1)], ALU.add),
                reads=[pOutp[hf], bXld[p]], writes=[bOutt[po]],
                waits=[P.bar(f"d_x{p}")])
        for hf in range(2):
            P.op("vector", lambda v, py=py, po=po, hf=hf: v.tensor_tensor(
                outt[po][:, 512 * hf:512 * (hf + 1)],
                outt[po][:, 512 * hf:512 * (hf + 1)],
                yt[py][:, 512 * hf:512 * (hf + 1)], ALU.add),
                reads=[bYt[py], bOutt[po]], writes=[bOutt[po]],
                waits=[P.bar(f"d_yg{py}")])

        P.op("sync", lambda s, i=i, po=po: s.dma_start(
            out=out_d[128 * i:128 * (i + 1), :], in_=outt[po][:]),
            reads=[bOutt[po]], dma=True, sem=f"d_out{po}")
        if i + 3 < NT:
            p3_loads(i + 3)

    P.op("vector", lambda v: v.engine_nop(),
         waits=[P.bar("d_out0"), P.bar("d_out1")], inc=False)
    P.build(st)
    st.close()
    return nc


def pack_ffn(w1, b1, w2, b2):
    E, C, H = 32, 1024, 256
    NC8 = C // 128
    w1 = np.asarray(w1, dtype=np.float32)
    b1 = np.asarray(b1, dtype=np.float32)
    w2 = np.asarray(w2, dtype=np.float32)
    b2 = np.asarray(b2, dtype=np.float32)
    w1p = w1.reshape(E, NC8, 128, H).transpose(0, 2, 1, 3).reshape(E, 128, NC8 * H)
    w2p = w2.reshape(E, 2, 128, C).transpose(0, 2, 1, 3).reshape(E, 128, 2 * C)
    b1p = b1.reshape(E, 2, 128).transpose(0, 2, 1)
    b2p = b2.reshape(E, NC8, 128).transpose(0, 2, 1)
    wpk = np.concatenate([w1p, w2p, b1p, b2p], axis=2).reshape(E * 128, 4 * C + 2 + NC8)
    return {"wpk": np.ascontiguousarray(wpk), "b2row": np.ascontiguousarray(b2)}


def make_consts(T=4096, SEG=256):
    E = 32
    NSEG = T // SEG + E
    NPT = T // SEG
    bf16 = ml_dtypes.bfloat16
    ident = np.eye(128, dtype=np.float32)
    k = np.arange(128)
    utmask = (k[:, None] <= k[None, :]).astype(np.float32)
    ustrict = (k[:, None] < k[None, :]).astype(np.float32)
    e = np.arange(E)
    u32 = (e[:, None] < e[None, :]).astype(np.float32)
    return {
        "identbf": ident.astype(bf16),
        "identf": ident,
        "utmask": utmask.astype(bf16),
        "iota32": np.tile(np.arange(E, dtype=np.float32), (128, 1)),
        "iotaE": e.astype(np.float32).reshape(E, 1),
        "iotaP": np.arange(128, dtype=np.float32).reshape(128, 1),
        "u32": u32,
        "ustrict": ustrict.astype(bf16),
        "thrP": np.tile((np.arange(NPT) * SEG).astype(np.float32), (E, 1)),
        "thrS": np.tile((np.arange(NSEG) * SEG).astype(np.float32), (E, 1)),
    }


def kernel(L0_attn_ids, L0_attn_qkv_w, L0_attn_qkv_b, L0_attn_proj_w, L0_attn_proj_b,
           L0_ffn_router_w, L0_ffn_router_b, L0_ffn_weights1, L0_ffn_bias1,
           L0_ffn_weights2, L0_ffn_bias2):
    T, C, E, H = 4096, 1024, 32, 256
    SEG = 128
    x = np.ascontiguousarray(np.asarray(L0_attn_ids, dtype=np.float32))
    weights = {
        "qkv_w": np.ascontiguousarray(np.asarray(L0_attn_qkv_w, dtype=np.float32)),
        "qkv_b": np.asarray(L0_attn_qkv_b, dtype=np.float32).reshape(1, 192),
        "proj_w": np.ascontiguousarray(np.asarray(L0_attn_proj_w, dtype=np.float32)),
        "proj_b": np.asarray(L0_attn_proj_b, dtype=np.float32).reshape(1, C),
        "router_w": np.ascontiguousarray(np.asarray(L0_ffn_router_w, dtype=np.float32)),
        "router_b": np.asarray(L0_ffn_router_b, dtype=np.float32).reshape(1, E),
    }
    weights.update(pack_ffn(L0_ffn_weights1, L0_ffn_bias1,
                            L0_ffn_weights2, L0_ffn_bias2))
    weights.update(make_consts(T, SEG))

    nc = bass.Bass("TRN2", target_bir_lowering=False)
    build_program(nc, T=T, SEG=SEG)

    in_maps = []
    for b in range(8):
        m = {"x": x[b]}
        m.update(weights)
        in_maps.append(m)
    res = run_bass_kernel_spmd(nc, in_maps, core_ids=list(range(8)))
    return np.stack([r["out"] for r in res.results], axis=0)
